# version 03
# baseline (speedup 1.0000x reference)
import numpy as np
import concourse.bass as bass
import concourse.mybir as mybir
from concourse.alu_op_type import AluOpType
from concourse.bass_utils import run_bass_kernel_spmd

F32 = mybir.dt.float32
BF16 = mybir.dt.bfloat16
AF = mybir.ActivationFunctionType
AX = mybir.AxisListType
A = AluOpType

NB = 2
T = 4096
D = 1024
H = 16
HD = 64
E = 8
TIL = 32
CH = 1024
NCH = T // CH
EPS = 1e-5

MB_C = 0
OST_C = 0
VW_C = 4096
RS_C = 6176
WO_C = 0
WI_C = 8224
GBC_C = 10272
AIN_C0 = 0
AIN_C1 = 4096
YIN_C0 = 8192
YIN_C1 = 10240
TR_C = 12320
GEN_W = TR_C + T

ENGS = ("sync", "gpsimd", "tensor", "scalar", "vector")
EK = {"sync": "X", "gpsimd": "G", "tensor": "PE", "scalar": "ACT", "vector": "DVE"}


def build_program(phases="ABCD", dbg=False):
    nc = bass.Bass("TRN2", target_bir_lowering=False, debug=False)
    x = nc.dram_tensor("x", [NB, T, D], F32, kind="ExternalInput").ap()
    wih = nc.dram_tensor("wih", [E, 32, 128, 1024], F32, kind="ExternalInput").ap()
    woh = nc.dram_tensor("woh", [E, 8, 128, 4096], F32, kind="ExternalInput").ap()
    wgh = nc.dram_tensor("wgh", [128, 64], F32, kind="ExternalInput").ap()
    ident_in = nc.dram_tensor("ident", [128, 128], F32, kind="ExternalInput").ap()
    oh_in = nc.dram_tensor("oh", [8, 1024], F32, kind="ExternalInput").ap()
    cones_in = nc.dram_tensor("cones", [128, 1], F32, kind="ExternalInput").ap()
    crow_in = nc.dram_tensor("crow", [1, 128], F32, kind="ExternalInput").ap()
    cnrow_in = nc.dram_tensor("cnrow", [1, 128], F32, kind="ExternalInput").ap()
    out = nc.dram_tensor("out", [NB, T, D], F32, kind="ExternalOutput").ap()
    okind = "ExternalOutput" if dbg else "Internal"
    attnT = nc.dram_tensor("attnT", [65, H, T], BF16, kind=okind).ap()
    yT = nc.dram_tensor("yT", [D, T], BF16, kind=okind).ap()
    gscr = nc.dram_tensor("gscr", [16, 1], F32, kind="Internal").ap()
    dbg_t = {}
    if dbg:
        for nm, shp, dt in (("dbg_xn", [128, TIL * D], BF16), ("dbg_xnt", [128, 8 * T], BF16),
                            ("dbg_lgt", [128, 256], F32), ("dbg_tr", [128, T], BF16),
                            ("dbg_gb", [128, 16], F32), ("dbg_w16", [128, 512], F32)):
            dbg_t[nm] = nc.dram_tensor(nm, shp, dt, kind="ExternalOutput").ap()

    plan = {e: [] for e in ENGS}
    cnt = {"X": 0, "G": 0, "WI": 0, "SO": 0, "PE": 0, "ACT": 0, "DVE": 0}
    last = {"ACT": 0, "DVE": 0}

    def em(eng, fn, waits=(), inc=False, key=None):
        key = key or EK[eng]
        amt = 16 if key in ("X", "G", "WI", "SO") else 1
        wl = list(waits)
        if key in ("ACT", "DVE"):
            inc = True
            if last[key] > 0:
                wl.append((key, last[key]))
        mil = None
        if inc:
            cnt[key] += amt
            mil = cnt[key]
            if key in ("ACT", "DVE"):
                last[key] = mil
        wd = {}
        for k, v in wl:
            if v > 0:
                wd[k] = max(wd.get(k, 0), v)
        wt = tuple(wd.items())

        def run(e, s, fn=fn, wt=wt, inc=inc, key=key, amt=amt):
            for k, v in wt:
                e.wait_ge(s[k], v)
            instr = fn(e)
            if inc:
                instr.then_inc(s[key], amt)
        plan[eng].append(run)
        return mil

    import contextlib
    st_ = contextlib.ExitStack()
    with st_:
        def _sb(name, shape, dt):
            return st_.enter_context(nc.sbuf_tensor(name, shape, dt))
        XNT = _sb("XNT", [128, 8 * T], BF16)
        BIG = _sb("BIG", [128, TIL * D], BF16)
        GEN = _sb("GEN", [128, GEN_W], BF16)
        U2 = _sb("U2", [128, 2048], F32)
        XT = _sb("XT", [128, 2048], F32)
        YAC = _sb("YAC", [128, 8 * CH], BF16)
        W16 = _sb("W16", [128, 512], F32)
        CMB = _sb("CMB", [128, 48], BF16)
        LGT = _sb("LGT", [128, 256], F32)
        STT = _sb("STT", [128, 16], F32)
        UWT = _sb("UWT", [128, 64], F32)
        MR1 = _sb("MR1", [1, 1024], BF16)
        GAM = _sb("GAM", [128, 24], F32)
        GB = _sb("GB", [128, 16], F32)
        GRW = _sb("GRW", [1, 16], BF16)
        RCP = _sb("RCP", [128, 16], F32)
        IDN = _sb("IDN", [128, 128], BF16)
        OH = _sb("OH", [8, 1024], BF16)
        WG = _sb("WG", [128, 64], BF16)
        CON1 = _sb("CON1", [128, 1], BF16)
        CROW = _sb("CROW", [1, 128], BF16)
        CNR = _sb("CNR", [1, 128], BF16)
        PSF = st_.enter_context(nc.psum_tensor("PSF", [128, 3072], F32))
        PSB = st_.enter_context(nc.psum_tensor("PSB", [128, 2048], BF16))
        SX = st_.enter_context(nc.semaphore("SX"))
        SG = st_.enter_context(nc.semaphore("SG"))
        SPE = st_.enter_context(nc.semaphore("SPE"))
        SACT = st_.enter_context(nc.semaphore("SACT"))
        SDVE = st_.enter_context(nc.semaphore("SDVE"))
        SWI = st_.enter_context(nc.semaphore("SWI"))
        SO = st_.enter_context(nc.semaphore("SO"))
        block = st_.enter_context(nc.Block())
        sems = {"X": SX, "G": SG, "PE": SPE, "ACT": SACT, "DVE": SDVE, "WI": SWI, "SO": SO}

        for dst, src in ((IDN, ident_in), (OH, oh_in), (WG, wgh), (CON1, cones_in),
                         (CROW, crow_in), (CNR, cnrow_in)):
            consts_done = em("gpsimd", lambda e, d=dst, sc=src: e.dma_start(out=d[:], in_=sc), inc=True)

        prev = {"out_x": 0, "d_dve": 0, "c_pe": 0, "c_dve": 0}

        for b in range(NB):
            xmil = [0] * TIL
            xnmil = [0] * TIL
            xntmil = [0] * TIL
            trmil = [0] * TIL
            lgmil = [0] * TIL
            for i in range(TIL):
                sl = i % 2
                waits = [("DVE", xnmil[i - 2] if i >= 2 else 0)]
                if b > 0 and i < 2:
                    waits += [("SO", prev["out_x"]), ("DVE", prev["d_dve"])]
                xmil[i] = em("sync", lambda e, sl=sl, bb=b, ii=i: e.dma_start(
                    out=XT[:, sl * 1024:(sl + 1) * 1024], in_=x[bb, ii * 128:(ii + 1) * 128, :]),
                    waits=waits, inc=True)
                xt = XT[:, sl * 1024:(sl + 1) * 1024]
                em("vector", lambda e, xt=xt: e.reduce_sum(STT[:, 0:1], xt, axis=AX.X),
                   waits=[("X", xmil[i])])
                em("vector", lambda e, xt=xt: e.tensor_tensor(U2[:, 0:1024], xt, xt, A.mult))
                em("vector", lambda e: e.reduce_sum(STT[:, 1:2], U2[:, 0:1024], axis=AX.X))
                em("vector", lambda e: e.tensor_scalar(STT[:, 2:3], STT[:, 0:1], 1.0 / D, None, A.mult, A.bypass))
                em("vector", lambda e: e.tensor_scalar(STT[:, 3:4], STT[:, 1:2], 1.0 / D, EPS, A.mult, A.add))
                em("vector", lambda e: e.tensor_tensor(STT[:, 4:5], STT[:, 2:3], STT[:, 2:3], A.mult))
                var_m = em("vector", lambda e: e.tensor_tensor(STT[:, 5:6], STT[:, 3:4], STT[:, 4:5], A.subtract))
                em("scalar", lambda e: e.activation(STT[:, 6:7], STT[:, 5:6], AF.Ln),
                   waits=[("DVE", var_m)])
                rstd_m = em("scalar", lambda e: e.activation(STT[:, 7:8], STT[:, 6:7], AF.Exp, scale=-0.5))
                xnmil[i] = em("vector", lambda e, xt=xt, ii=i: e.tensor_scalar(
                    BIG[:, ii * D:(ii + 1) * D], xt, STT[:, 2:3], STT[:, 7:8], A.subtract, A.mult),
                    waits=[("ACT", rstd_m)])
                twaits = [("DVE", xnmil[i]), ("G", consts_done),
                          ("DVE", xntmil[i - 2] if i >= 2 else 0)]
                if b > 0 and i < 2:
                    twaits.append(("PE", prev["c_pe"]))
                for ss in range(8):
                    trmil[i] = em("tensor", lambda e, ii=i, ss=ss: e.transpose(
                        PSB[:, ((ii % 2) * 8 + ss) * 128:((ii % 2) * 8 + ss + 1) * 128],
                        BIG[:, ii * D + ss * 128: ii * D + (ss + 1) * 128], IDN[:]),
                        waits=twaits if ss == 0 else (), inc=(ss == 7))
                xntmil[i] = em("vector", lambda e, ii=i: e.tensor_copy(
                    XNT[:].rearrange("p (s t) -> p s t", s=8)[:, :, ii * 128:(ii + 1) * 128],
                    PSB[:, (ii % 2) * 1024:((ii % 2) + 1) * 1024].rearrange("p (s t) -> p s t", s=8)),
                    waits=[("PE", trmil[i])])
                for ss in range(8):
                    lgmil[i] = em("tensor", lambda e, ii=i, ss=ss: e.matmul(
                        PSF[:, 1024 + ii * 8:1024 + (ii + 1) * 8],
                        XNT[:, ss * T + ii * 128: ss * T + (ii + 1) * 128],
                        WG[:, ss * 8:(ss + 1) * 8], start=(ss == 0), stop=(ss == 7)),
                        waits=[("DVE", xntmil[i])] if ss == 0 else (), inc=(ss == 7))
                for hh in range(2):
                    m_acc_pe = em("tensor", lambda e, ii=i, hh=hh: e.matmul(
                        PSF[0:1, 1536 + hh * 512:1536 + (hh + 1) * 512], CON1[:],
                        BIG[:, ii * D + hh * 512: ii * D + hh * 512 + 512],
                        start=(ii == 0), stop=(ii == TIL - 1)),
                        inc=(hh == 1 and i == TIL - 1))
            lg_evac = em("vector", lambda e: e.tensor_copy(LGT[:], PSF[:, 1024:1280]),
                         waits=[("PE", lgmil[TIL - 1])])
            mr1_m = em("vector", lambda e: e.tensor_scalar(MR1[:], PSF[0:1, 1536:2560], 1.0 / T, None, A.mult, A.bypass),
                       waits=[("PE", m_acc_pe)])
            for hh in range(2):
                mbp = em("tensor", lambda e, hh=hh: e.matmul(
                    PSF[:, hh * 512:(hh + 1) * 512], CROW[:], MR1[0:1, hh * 512:(hh + 1) * 512],
                    start=True, stop=True),
                    waits=[("DVE", mr1_m)] if hh == 0 else (), inc=(hh == 1))
            mb_m = em("vector", lambda e: e.tensor_copy(GEN[:, MB_C:MB_C + 1024], PSF[:, 0:1024]),
                      waits=[("PE", mbp)])

            if dbg and b == 0:
                em("sync", lambda e: e.dma_start(out=dbg_t["dbg_xn"], in_=BIG[:]), waits=[("DVE", xnmil[TIL - 1])], inc=True)
                em("sync", lambda e: e.dma_start(out=dbg_t["dbg_xnt"], in_=XNT[:]), waits=[("DVE", xntmil[TIL - 1])], inc=True)
                em("sync", lambda e: e.dma_start(out=dbg_t["dbg_lgt"], in_=LGT[:]), waits=[("DVE", lg_evac)], inc=True)

            ctr_cp = [0] * TIL
            ctr_pe = [0] * TIL
            w16_last = 0
            for i in range(TIL):
                em("vector", lambda e, ii=i: e.tensor_tensor(
                    U2[:, 0:1024], BIG[:, ii * D:(ii + 1) * D], GEN[:, MB_C:MB_C + 1024], A.mult),
                    waits=[("DVE", mb_m)])
                u_m = em("vector", lambda e: e.reduce_sum(
                    UWT[:, 0:16], U2[:, 0:1024].rearrange("p (w c) -> p w c", c=64), axis=AX.X))
                em("vector", lambda e, ii=i: e.tensor_tensor(
                    U2[:, 0:1024], BIG[:, ii * D:(ii + 1) * D], BIG[:, ii * D:(ii + 1) * D], A.mult))
                n2_m = em("vector", lambda e: e.reduce_sum(
                    UWT[:, 16:32], U2[:, 0:1024].rearrange("p (w c) -> p w c", c=64), axis=AX.X))
                w_m = em("scalar", lambda e: e.activation(UWT[:, 32:48], UWT[:, 0:16], AF.Exp, scale=-1.0),
                         waits=[("DVE", u_m)])
                w16_last = em("vector", lambda e, ii=i: e.tensor_copy(W16[:, ii * 16:(ii + 1) * 16], UWT[:, 32:48]),
                              waits=[("ACT", w_m)])
                lg = LGT[:, i * 8:(i + 1) * 8]
                em("vector", lambda e, lg=lg: e.reduce_max(UWT[:, 48:49], lg, axis=AX.X))
                em("vector", lambda e, lg=lg: e.tensor_scalar(U2[:, 1024:1032], lg, UWT[:, 48:49], None, A.is_ge, A.bypass))
                em("vector", lambda e, lg=lg: e.scalar_tensor_tensor(U2[:, 1032:1040], U2[:, 1024:1032], -1e30, lg, A.mult, A.add))
                em("vector", lambda e: e.reduce_max(UWT[:, 49:50], U2[:, 1032:1040], axis=AX.X))
                em("vector", lambda e: e.tensor_scalar(U2[:, 1040:1048], U2[:, 1032:1040], UWT[:, 49:50], None, A.is_ge, A.bypass))
                d_m = em("vector", lambda e: e.tensor_tensor(UWT[:, 50:51], UWT[:, 49:50], UWT[:, 48:49], A.subtract))
                eact = em("scalar", lambda e: e.activation(UWT[:, 51:52], UWT[:, 50:51], AF.Exp),
                          waits=[("DVE", d_m)])
                em("vector", lambda e: e.tensor_scalar(UWT[:, 52:53], UWT[:, 51:52], 1.0, None, A.add, A.bypass),
                   waits=[("ACT", eact)])
                em("vector", lambda e: e.reciprocal(UWT[:, 53:54], UWT[:, 52:53]))
                em("vector", lambda e: e.tensor_tensor(UWT[:, 54:55], UWT[:, 51:52], UWT[:, 53:54], A.mult))
                em("vector", lambda e: e.tensor_scalar(U2[:, 1048:1056], U2[:, 1040:1048], UWT[:, 54:55], None, A.mult, A.bypass))
                em("vector", lambda e: e.scalar_tensor_tensor(CMB[:, 0:8], U2[:, 1024:1032], UWT[:, 53:54], U2[:, 1048:1056], A.mult, A.add),
                   waits=[("PE", ctr_pe[i - 1] if i >= 1 else 0)])
                n2c_m = em("vector", lambda e: e.tensor_copy(CMB[:, 32:48], UWT[:, 16:32]))
                ctr_pe[i] = em("tensor", lambda e, ii=i: e.transpose(
                    PSB[0:48, (ii % 2) * 128:(ii % 2) * 128 + 128],
                    CMB[:], IDN[:]),
                    waits=[("DVE", n2c_m), ("DVE", ctr_cp[i - 2] if i >= 2 else 0)], inc=True)
                ctr_cp[i] = em("vector", lambda e, ii=i: e.tensor_copy(
                    GEN[0:48, TR_C + ii * 128:TR_C + (ii + 1) * 128],
                    PSB[0:48, (ii % 2) * 128:(ii % 2) * 128 + 128]),
                    waits=[("PE", ctr_pe[i])])
            em("vector", lambda e: e.reduce_max(GAM[32:48, 4:5], GEN[32:48, TR_C:TR_C + T], axis=AX.X),
               waits=[("DVE", ctr_cp[TIL - 1])])
            gam_m = em("vector", lambda e: e.tensor_scalar(GAM[32:48, 16:17], GAM[32:48, 4:5], -70.0, None, A.add, A.bypass))
            gb_x = em("sync", lambda e: e.dma_start(out=gscr, in_=GAM[32:48, 16:17]),
                      waits=[("DVE", gam_m)], inc=True)
            gb_g = em("gpsimd", lambda e: e.dma_start(out=GRW[:], in_=gscr.rearrange("a b -> (a b)").unsqueeze(0)),
                      waits=[("X", gb_x)], inc=True)
            gbp = em("tensor", lambda e: e.matmul(PSF[:, 0:16], CNR[:], GRW[:], start=True, stop=True),
                     waits=[("G", gb_g), ("DVE", mb_m)], inc=True)
            gb_m = em("vector", lambda e: e.tensor_copy(GB[:], PSF[:, 0:16]),
                      waits=[("PE", gbp)])

            if dbg and b == 0:
                em("sync", lambda e: e.dma_start(out=dbg_t["dbg_tr"], in_=GEN[:, TR_C:TR_C + T]),
                   waits=[("DVE", ctr_cp[TIL - 1])], inc=True)
                em("sync", lambda e: e.dma_start(out=dbg_t["dbg_gb"], in_=GB[:]), waits=[("DVE", gb_m)], inc=True)
                em("sync", lambda e: e.dma_start(out=dbg_t["dbg_w16"], in_=W16[:]), waits=[("DVE", w16_last)], inc=True)

            if "B" not in phases:
                prev = {"out_x": cnt["X"], "d_dve": cnt["DVE"], "c_pe": cnt["PE"], "c_dve": cnt["DVE"]}
                continue

            prev_head_o_last = 0
            prev_head_oevac = mb_m
            attn_store = [0] * H
            for h in range(H):
                s0 = h // 2
                rbase = (h % 2) * 64
                vw_w = [("DVE", w16_last), ("PE", prev_head_o_last), ("X", attn_store[h - 1] if h >= 1 else 0)]
                for j in range(TIL):
                    em("vector", lambda e, j=j, h=h: e.tensor_scalar(
                        GEN[:, VW_C + j * 65:VW_C + j * 65 + 64],
                        BIG[:, j * D + h * HD: j * D + h * HD + 64],
                        W16[:, j * 16 + h: j * 16 + h + 1], None, A.mult, A.bypass),
                        waits=vw_w if j == 0 else ())
                vw_m = em("vector", lambda e, h=h: e.tensor_copy(
                    GEN[:, VW_C:VW_C + 65 * 32].rearrange("p (j c) -> p j c", c=65)[:, :, 64:65],
                    W16[:, 0:512].rearrange("p (j c) -> p j c", c=16)[:, :, h:h + 1]))
                NCB = T // 512
                NJJ = NCB * TIL
                smil = [0] * NJJ
                omil = [0] * NJJ
                expmil = [0] * NJJ
                oevac = [0] * NCB

                def emit_s(jj):
                    ch_, j_ = divmod(jj, TIL)
                    sw = [("ACT", expmil[jj - 4] if jj >= 4 else 0)]
                    if jj == 0:
                        sw += [("DVE", vw_m), ("DVE", gb_m)]
                    smil[jj] = em("tensor", lambda e, s0=s0, rb=rbase, j=j_, ch=ch_, jj=jj: e.matmul(
                        PSF[:, (jj % 4) * 512:(jj % 4) * 512 + 512],
                        XNT[rb:rb + 64, s0 * T + j * 128: s0 * T + (j + 1) * 128],
                        XNT[rb:rb + 64, s0 * T + ch * 512: s0 * T + ch * 512 + 512],
                        start=True, stop=True),
                        waits=sw, inc=True)
                for p_ in range(3):
                    emit_s(p_)
                for jj in range(NJJ):
                    ch, j = divmod(jj, TIL)
                    if jj + 3 < NJJ:
                        emit_s(jj + 3)
                    ew = [("PE", smil[jj]),
                          ("PE", omil[jj - 4] if jj >= 4 else prev_head_o_last)]
                    expmil[jj] = em("scalar", lambda e, jj=jj, h=h: e.activation(
                        GEN[:, RS_C + (jj % 4) * 512:RS_C + (jj % 4) * 512 + 512],
                        PSF[:, (jj % 4) * 512:(jj % 4) * 512 + 512],
                        AF.Exp, bias=GB[:, h: h + 1]),
                        waits=ew)
                    ow = [("ACT", expmil[jj])]
                    if j == 0:
                        ow.append(("DVE", oevac[ch - 2] if ch >= 2 else prev_head_oevac))
                    omil[jj] = em("tensor", lambda e, j=j, jj=jj, ch=ch: e.matmul(
                        PSF[0:65, 2048 + (ch % 2) * 512:2048 + (ch % 2) * 512 + 512],
                        GEN[:, VW_C + j * 65:VW_C + (j + 1) * 65],
                        GEN[:, RS_C + (jj % 4) * 512:RS_C + (jj % 4) * 512 + 512],
                        start=(j == 0), stop=(j == TIL - 1)),
                        waits=ow, inc=True)
                    if j == TIL - 1:
                        ovw = [("PE", omil[jj])]
                        if ch <= 1:
                            ovw.append(("X", attn_store[h - 1] if h >= 1 else 0))
                        oevac[ch] = em("vector", lambda e, ch=ch: e.tensor_copy(
                            GEN[0:65, OST_C + ch * 512:OST_C + (ch + 1) * 512],
                            PSF[0:65, 2048 + (ch % 2) * 512:2048 + (ch % 2) * 512 + 512]),
                            waits=ovw)
                prev_head_o_last = omil[NJJ - 1]
                prev_head_oevac = oevac[NCB - 1]
                attn_store[h] = em("sync", lambda e, h=h: e.dma_start(
                    out=attnT[:, h, :], in_=GEN[0:65, OST_C:OST_C + T]),
                    waits=[("DVE", oevac[NCB - 1])], inc=True)
            b_last_pe = prev_head_o_last
            b_store_x = attn_store[H - 1]

            if "C" not in phases:
                prev = {"out_x": cnt["X"], "d_dve": cnt["DVE"], "c_pe": cnt["PE"], "c_dve": cnt["DVE"]}
                continue

            bp01 = ("DVE", 0)
            bp23 = ("ACT", 0)
            bp45 = ("DVE", prev_head_oevac)
            wi_slot_free = [b_last_pe, b_last_pe]
            wo_slot_free = [b_last_pe, b_last_pe]
            gbc_slot_free = [b_last_pe, b_last_pe]
            gbc_slot_free_d = [0, 0]
            yst = 0
            p2_last_pe = b_last_pe
            p2_last_dve = 0
            for ch in range(NCH):
                for ee in range(E):
                    gw = [(bp45[0], bp45[1]), ("DVE", ctr_cp[TIL - 1]), ("X", b_store_x)]
                    for hh in range(2):
                        gbp = em("tensor", lambda e, ee=ee, ch=ch, hh=hh: e.matmul(
                            PSF[:, 2048 + hh * 512:2048 + hh * 512 + 512],
                            OH[:, ee * 128:(ee + 1) * 128],
                            GEN[0:8, TR_C + ch * CH + hh * 512:TR_C + ch * CH + hh * 512 + 512],
                            start=True, stop=True),
                            waits=gw if hh == 0 else (), inc=(hh == 1))
                    gbc_m = em("vector", lambda e, ee=ee: e.tensor_copy(
                        GEN[:, GBC_C + (ee % 2) * 1024:GBC_C + (ee % 2) * 1024 + 1024], PSF[:, 2048:3072]),
                        waits=[("PE", gbp), ("DVE", gbc_slot_free_d[ee % 2])])
                    bp45 = ("DVE", gbc_m)
                    p1evac_v = 0
                    p1evac_a = 0
                    p1mil = [0] * 32
                    for ht in range(32):
                        wdm = em("gpsimd", lambda e, ee=ee, ht=ht: e.dma_start(
                            out=GEN[:, WI_C + (ht % 2) * 1024:WI_C + (ht % 2) * 1024 + 1024], in_=wih[ee, ht]),
                            waits=[("PE", wi_slot_free[ht % 2]), ("X", b_store_x)], inc=True, key="WI")
                        bp = bp01 if ht % 2 == 0 else bp23
                        mw = [("WI", wdm), (bp[0], bp[1])]
                        for k in range(8):
                            for hh in range(2):
                                p1mil[ht] = em("tensor", lambda e, ht=ht, k=k, hh=hh, ch=ch: e.matmul(
                                    PSF[:, (ht % 2) * 1024 + hh * 512:(ht % 2) * 1024 + hh * 512 + 512],
                                    GEN[:, WI_C + (ht % 2) * 1024 + k * 128:WI_C + (ht % 2) * 1024 + (k + 1) * 128],
                                    XNT[:, k * T + ch * CH + hh * 512: k * T + ch * CH + hh * 512 + 512],
                                    start=(k == 0), stop=(k == 7)),
                                    waits=mw if (k == 0 and hh == 0) else (), inc=(k == 7 and hh == 1))
                        wi_slot_free[ht % 2] = p1mil[ht]
                        evw = [("PE", p1mil[ht])]
                        if ht < 2:
                            evw.append(("PE", p2_last_pe))
                        if ht % 2 == 0:
                            p1evac_v = em("vector", lambda e, ht=ht: e.tensor_copy(
                                BIG[:, ht * 1024:(ht + 1) * 1024], PSF[:, 0:1024]), waits=evw)
                            bp01 = ("DVE", p1evac_v)
                        else:
                            p1evac_a = em("scalar", lambda e, ht=ht: e.copy(
                                BIG[:, ht * 1024:(ht + 1) * 1024], PSF[:, 1024:2048]), waits=evw)
                            bp23 = ("ACT", p1evac_a)
                    for dt in range(8):
                        wdm = em("gpsimd", lambda e, ee=ee, dt=dt: e.dma_start(
                            out=GEN[:, WO_C + (dt % 2) * 4096:WO_C + (dt % 2) * 4096 + 4096], in_=woh[ee, dt]),
                            waits=[("PE", wo_slot_free[dt % 2]), ("X", b_store_x)], inc=True)
                        bp = bp45 if dt % 2 == 0 else bp23
                        mw = [("G", wdm), (bp[0], bp[1]),
                              ("DVE", p1evac_v), ("ACT", p1evac_a)]
                        pcol = 2048 if dt % 2 == 0 else 1024
                        p2m = 0
                        for sth in range(32):
                            for hh in range(2):
                                p2m = em("tensor", lambda e, dt=dt, sth=sth, hh=hh, pcol=pcol: e.matmul(
                                    PSF[:, pcol + hh * 512:pcol + hh * 512 + 512],
                                    GEN[:, WO_C + (dt % 2) * 4096 + sth * 128:WO_C + (dt % 2) * 4096 + (sth + 1) * 128],
                                    BIG[:, sth * 1024 + hh * 512: sth * 1024 + hh * 512 + 512],
                                    start=(sth == 0), stop=(sth == 31)),
                                    waits=mw if (sth == 0 and hh == 0) else (), inc=(sth == 31 and hh == 1))
                        wo_slot_free[dt % 2] = p2m
                        p2_last_pe = p2m
                        ew = [("PE", p2m), ("DVE", gbc_m)]
                        if ee == 0:
                            ew.append(("X", yst))
                        em("vector", lambda e, pcol=pcol, ee=ee: e.tensor_tensor(
                            U2[:, 0:1024], PSF[:, pcol:pcol + 1024],
                            GEN[:, GBC_C + (ee % 2) * 1024:GBC_C + (ee % 2) * 1024 + 1024], A.mult),
                            waits=ew)
                        if ee == 0:
                            p2ev = em("vector", lambda e, dt=dt: e.tensor_copy(
                                YAC[:, dt * 1024:(dt + 1) * 1024], U2[:, 0:1024]))
                        else:
                            p2ev = em("vector", lambda e, dt=dt: e.tensor_tensor(
                                YAC[:, dt * 1024:(dt + 1) * 1024], YAC[:, dt * 1024:(dt + 1) * 1024],
                                U2[:, 0:1024], A.add))
                        if dt % 2 == 0:
                            bp45 = ("DVE", p2ev)
                        else:
                            bp23 = ("DVE", p2ev)
                        p2_last_dve = p2ev
                    gbc_slot_free_d[ee % 2] = p2_last_dve
                yst = em("sync", lambda e, ch=ch: e.dma_start(
                    out=yT[:, ch * CH:(ch + 1) * CH].rearrange("(dt p) t -> p dt t", p=128),
                    in_=YAC[:].rearrange("p (dt t) -> p dt t", t=CH)),
                    waits=[("DVE", p2_last_dve)], inc=True)
            c_last_pe = p2_last_pe
            c_last_dve = p2_last_dve
            c_last_x = yst

            if "D" not in phases:
                prev = {"out_x": cnt["X"], "d_dve": cnt["DVE"], "c_pe": cnt["PE"], "c_dve": cnt["DVE"]}
                continue

            attnT_r = attnT
            ain_mil = [0, 0]
            yin_mil = [0, 0]
            ain_free = [c_last_pe, c_last_pe]
            yin_free = [c_last_pe, c_last_pe]
            acc_free = [0, 0]
            xt_free = [c_last_dve, c_last_dve]
            d_dve = c_last_dve
            out_x = c_last_x
            atr_last = [0, 0]
            ytr_last = [0, 0]
            yadd_last = [0, 0]
            AIN_CS = (AIN_C0, AIN_C1)
            YIN_CS = (YIN_C0, YIN_C1)
            for pp in range(16):
                ps = pp % 2
                ain_mil[ps] = em("sync", lambda e, pp=pp, ps=ps: e.dma_start(
                    out=GEN[0:65, AIN_CS[ps]:AIN_CS[ps] + 4096].rearrange("r (h t) -> r h t", t=256),
                    in_=attnT_r[:, :, pp * 256:(pp + 1) * 256]),
                    waits=[("PE", ain_free[ps]), ("X", c_last_x), ("DVE", c_last_dve)], inc=True)
                yin_mil[ps] = em("sync", lambda e, pp=pp, ps=ps: e.dma_start(
                    out=GEN[:, YIN_CS[ps]:YIN_CS[ps] + 2048].rearrange("p (s t) -> p s t", t=256),
                    in_=yT[:, pp * 256:(pp + 1) * 256].rearrange("(s p) t -> p s t", p=128)),
                    waits=[("PE", yin_free[ps]), ("X", c_last_x), ("DVE", c_last_dve)], inc=True)
                for t2 in range(2):
                    i = pp * 2 + t2
                    isl = i % 2
                    xm = em("sync", lambda e, bb=b, ii=i, isl=isl: e.dma_start(
                        out=XT[:, isl * 1024:(isl + 1) * 1024], in_=x[bb, ii * 128:(ii + 1) * 128, :]),
                        waits=[("DVE", xt_free[isl])], inc=True)
                    tw = [("X", xm), ("DVE", yadd_last[1 - isl] if i >= 1 else 0),
                          ("DVE", d_dve if i == 0 else 0)]
                    atr = 0
                    for h in range(H):
                        atr = em("tensor", lambda e, h=h, ps=ps, t2=t2: e.transpose(
                            PSB[:, h * 128:h * 128 + 65],
                            GEN[0:65, AIN_CS[ps] + h * 256 + t2 * 128:AIN_CS[ps] + h * 256 + t2 * 128 + 128],
                            IDN[0:65, 0:65]),
                            waits=tw if h == 0 else (), inc=(h == H - 1))
                    atr_last[isl] = atr
                    ain_free[ps] = atr
                    for h in range(H):
                        em("vector", lambda e, h=h: e.reciprocal(RCP[:, h:h + 1], PSB[:, h * 128 + 64:h * 128 + 65]),
                           waits=[("PE", atr)] if h == 0 else ())
                    st_m = 0
                    for h in range(H):
                        st_m = em("vector", lambda e, h=h, isl=isl: e.scalar_tensor_tensor(
                            U2[:, isl * 1024 + h * 64: isl * 1024 + (h + 1) * 64],
                            PSB[:, h * 128:h * 128 + 64], RCP[:, h:h + 1],
                            XT[:, isl * 1024 + h * 64: isl * 1024 + (h + 1) * 64], A.mult, A.add),
                            waits=[("X", xm), ("SO", acc_free[isl])] if h == 0 else ())
                    xt_free[isl] = st_m
                    ytr = 0
                    for sth in range(8):
                        ytr = em("tensor", lambda e, sth=sth, ps=ps, t2=t2: e.transpose(
                            PSB[:, sth * 128:(sth + 1) * 128],
                            GEN[:, YIN_CS[ps] + sth * 256 + t2 * 128:YIN_CS[ps] + sth * 256 + t2 * 128 + 128],
                            IDN[:]),
                            waits=[("DVE", st_m), ("X", xm)] if sth == 0 else (), inc=(sth == 7))
                    ytr_last[isl] = ytr
                    yin_free[ps] = ytr
                    ya = 0
                    for sth in range(8):
                        ya = em("vector", lambda e, sth=sth, isl=isl: e.tensor_tensor(
                            U2[:, isl * 1024 + sth * 128: isl * 1024 + (sth + 1) * 128],
                            U2[:, isl * 1024 + sth * 128: isl * 1024 + (sth + 1) * 128],
                            PSB[:, sth * 128:(sth + 1) * 128], A.add),
                            waits=[("PE", ytr)] if sth == 0 else ())
                    yadd_last[isl] = ya
                    d_dve = ya
                    acc_free[isl] = em("sync", lambda e, bb=b, ii=i, isl=isl: e.dma_start(
                        out=out[bb, ii * 128:(ii + 1) * 128, :], in_=U2[:, isl * 1024:(isl + 1) * 1024]),
                        waits=[("DVE", ya)], inc=True, key="SO")
                    out_x = acc_free[isl]
            prev = {"out_x": out_x, "d_dve": d_dve, "c_pe": cnt["PE"], "c_dve": cnt["DVE"]}

        em("sync", lambda e: e.nop(), waits=[("X", cnt["X"]), ("SO", cnt["SO"])])
        em("gpsimd", lambda e: e.nop(), waits=[("G", cnt["G"]), ("WI", cnt["WI"])])

        @block.sync
        def _(eng):
            for f in plan["sync"]:
                f(eng, sems)

        @block.gpsimd
        def _(eng):
            for f in plan["gpsimd"]:
                f(eng, sems)

        @block.tensor
        def _(eng):
            for f in plan["tensor"]:
                f(eng, sems)

        @block.scalar
        def _(eng):
            for f in plan["scalar"]:
                f(eng, sems)

        @block.vector
        def _(eng):
            for f in plan["vector"]:
                f(eng, sems)

    return nc


def host_inputs(L0_attn_x, L0_ffn_w_gate, L0_ffn_w_in, L0_ffn_w_out):
    xf = np.ascontiguousarray(np.asarray(L0_attn_x, dtype=np.float32))
    wg = np.asarray(L0_ffn_w_gate, dtype=np.float32)
    wi = np.asarray(L0_ffn_w_in, dtype=np.float32)
    wo = np.asarray(L0_ffn_w_out, dtype=np.float32)
    WIH = np.ascontiguousarray(wi.reshape(E, 32, 128, 8, 128).transpose(0, 1, 4, 3, 2)).reshape(E, 32, 128, 1024)
    WOH = np.ascontiguousarray(wo.reshape(E, 8, 128, 32, 128).transpose(0, 1, 4, 3, 2)).reshape(E, 8, 128, 4096)
    WGH = np.ascontiguousarray(wg.reshape(8, 8, 128).transpose(2, 1, 0)).reshape(128, 64)
    ident = np.eye(128, dtype=np.float32)
    oh = np.zeros((8, 1024), dtype=np.float32)
    for e in range(8):
        oh[e, e * 128:(e + 1) * 128] = 1.0
    cones = np.ones((128, 1), dtype=np.float32)
    crow = np.ones((1, 128), dtype=np.float32)
    cnrow = -np.ones((1, 128), dtype=np.float32)
    shards = [np.ascontiguousarray(xf[2 * c:2 * c + 2]) for c in range(8)]
    in_maps = [{"x": shards[c], "wih": WIH, "woh": WOH, "wgh": WGH, "ident": ident,
                "oh": oh, "cones": cones, "crow": crow, "cnrow": cnrow} for c in range(8)]
    return in_maps


def kernel(L0_attn_x, L0_ffn_w_gate, L0_ffn_w_in, L0_ffn_w_out):
    in_maps = host_inputs(L0_attn_x, L0_ffn_w_gate, L0_ffn_w_in, L0_ffn_w_out)
    nc = build_program()
    res = run_bass_kernel_spmd(nc, in_maps, core_ids=list(range(8)))
    return np.concatenate([np.asarray(r["out"]) for r in res.results], axis=0)


# version 04
# speedup vs baseline: 1.1365x; 1.1365x over previous
import numpy as np
import concourse.bass as bass
import concourse.mybir as mybir
from concourse.alu_op_type import AluOpType
from concourse.bass_utils import run_bass_kernel_spmd

F32 = mybir.dt.float32
BF16 = mybir.dt.bfloat16
AF = mybir.ActivationFunctionType
AX = mybir.AxisListType
A = AluOpType

NB = 2
T = 4096
D = 1024
H = 16
HD = 64
E = 8
TIL = 32
CH = 1024
NCH = T // CH
EPS = 1e-5

MB_C = 0
OST_C = 0
VW_C = 4096
RS_C = 6176
WO_C = 0
WI_C = 8224
GBC_C = 10272
AIN_C0 = 0
AIN_C1 = 4096
YIN_C0 = 8192
YIN_C1 = 10240
TR_C = 12320
GEN_W = TR_C + T

ENGS = ("sync", "gpsimd", "tensor", "scalar", "vector")
EK = {"sync": "X", "gpsimd": "G", "tensor": "PE", "scalar": "ACT", "vector": "DVE"}


def build_program(phases="ABCD", dbg=False):
    nc = bass.Bass("TRN2", target_bir_lowering=False, debug=False)
    x = nc.dram_tensor("x", [NB, T, D], F32, kind="ExternalInput").ap()
    wih = nc.dram_tensor("wih", [E, 32, 128, 1024], F32, kind="ExternalInput").ap()
    woh = nc.dram_tensor("woh", [E, 8, 128, 4096], F32, kind="ExternalInput").ap()
    wgh = nc.dram_tensor("wgh", [128, 64], F32, kind="ExternalInput").ap()
    ident_in = nc.dram_tensor("ident", [128, 128], F32, kind="ExternalInput").ap()
    oh_in = nc.dram_tensor("oh", [8, 1024], F32, kind="ExternalInput").ap()
    cones_in = nc.dram_tensor("cones", [128, 1], F32, kind="ExternalInput").ap()
    crow_in = nc.dram_tensor("crow", [1, 128], F32, kind="ExternalInput").ap()
    cnrow_in = nc.dram_tensor("cnrow", [1, 128], F32, kind="ExternalInput").ap()
    out = nc.dram_tensor("out", [NB, T, D], F32, kind="ExternalOutput").ap()
    okind = "ExternalOutput" if dbg else "Internal"
    attnT = nc.dram_tensor("attnT", [65, H, T], BF16, kind=okind).ap()
    yT = nc.dram_tensor("yT", [D, T], BF16, kind=okind).ap()
    gscr = nc.dram_tensor("gscr", [16, 1], F32, kind="Internal").ap()
    dbg_t = {}
    if dbg:
        for nm, shp, dt in (("dbg_xn", [128, TIL * D], BF16), ("dbg_xnt", [128, 8 * T], BF16),
                            ("dbg_lgt", [128, 256], F32), ("dbg_tr", [128, T], BF16),
                            ("dbg_gb", [128, 16], F32), ("dbg_w16", [128, 512], F32)):
            dbg_t[nm] = nc.dram_tensor(nm, shp, dt, kind="ExternalOutput").ap()

    plan = {e: [] for e in ENGS}
    cnt = {"X": 0, "G": 0, "WI": 0, "SO": 0, "PE": 0, "ACT": 0, "DVE": 0}
    last = {"ACT": 0, "DVE": 0}

    def em(eng, fn, waits=(), inc=False, key=None):
        key = key or EK[eng]
        amt = 16 if key in ("X", "G", "WI", "SO") else 1
        wl = list(waits)
        if key in ("ACT", "DVE"):
            inc = True
            if last[key] > 0:
                wl.append((key, last[key]))
        mil = None
        if inc:
            cnt[key] += amt
            mil = cnt[key]
            if key in ("ACT", "DVE"):
                last[key] = mil
        wd = {}
        for k, v in wl:
            if v > 0:
                wd[k] = max(wd.get(k, 0), v)
        wt = tuple(wd.items())

        def run(e, s, fn=fn, wt=wt, inc=inc, key=key, amt=amt):
            for k, v in wt:
                e.wait_ge(s[k], v)
            instr = fn(e)
            if inc:
                instr.then_inc(s[key], amt)
        plan[eng].append(run)
        return mil

    import contextlib
    st_ = contextlib.ExitStack()
    with st_:
        def _sb(name, shape, dt):
            return st_.enter_context(nc.sbuf_tensor(name, shape, dt))
        XNT = _sb("XNT", [128, 8 * T], BF16)
        BIG = _sb("BIG", [128, TIL * D], BF16)
        GEN = _sb("GEN", [128, GEN_W], BF16)
        U2 = _sb("U2", [128, 2048], F32)
        XT = _sb("XT", [128, 2048], F32)
        YAC = _sb("YAC", [128, 8 * CH], BF16)
        W16 = _sb("W16", [128, 512], F32)
        CMB = _sb("CMB", [128, 48], BF16)
        LGT = _sb("LGT", [128, 256], F32)
        STT = _sb("STT", [128, 16], F32)
        UWT = _sb("UWT", [128, 64], F32)
        MR1 = _sb("MR1", [1, 1024], BF16)
        GAM = _sb("GAM", [128, 24], F32)
        GB = _sb("GB", [128, 16], F32)
        GRW = _sb("GRW", [1, 16], BF16)
        RCP = _sb("RCP", [128, 16], F32)
        IDN = _sb("IDN", [128, 128], BF16)
        OH = _sb("OH", [8, 1024], BF16)
        WG = _sb("WG", [128, 64], BF16)
        CON1 = _sb("CON1", [128, 1], BF16)
        CROW = _sb("CROW", [1, 128], BF16)
        CNR = _sb("CNR", [1, 128], BF16)
        PSF = st_.enter_context(nc.psum_tensor("PSF", [128, 3072], F32))
        PSB = st_.enter_context(nc.psum_tensor("PSB", [128, 2048], BF16))
        SX = st_.enter_context(nc.semaphore("SX"))
        SG = st_.enter_context(nc.semaphore("SG"))
        SPE = st_.enter_context(nc.semaphore("SPE"))
        SACT = st_.enter_context(nc.semaphore("SACT"))
        SDVE = st_.enter_context(nc.semaphore("SDVE"))
        SWI = st_.enter_context(nc.semaphore("SWI"))
        SO = st_.enter_context(nc.semaphore("SO"))
        block = st_.enter_context(nc.Block())
        sems = {"X": SX, "G": SG, "PE": SPE, "ACT": SACT, "DVE": SDVE, "WI": SWI, "SO": SO}

        for dst, src in ((IDN, ident_in), (OH, oh_in), (WG, wgh), (CON1, cones_in),
                         (CROW, crow_in), (CNR, cnrow_in)):
            consts_done = em("gpsimd", lambda e, d=dst, sc=src: e.dma_start(out=d[:], in_=sc), inc=True)

        prev = {"out_x": 0, "d_dve": 0, "c_pe": 0, "c_dve": 0}

        for b in range(NB):
            xmil = [0] * TIL
            xnmil = [0] * TIL
            xntmil = [0] * TIL
            trmil = [0] * TIL
            lgmil = [0] * TIL
            for i in range(TIL):
                sl = i % 2
                waits = [("DVE", xnmil[i - 2] if i >= 2 else 0)]
                if b > 0 and i < 2:
                    waits += [("SO", prev["out_x"]), ("DVE", prev["d_dve"])]
                xmil[i] = em("sync", lambda e, sl=sl, bb=b, ii=i: e.dma_start(
                    out=XT[:, sl * 1024:(sl + 1) * 1024], in_=x[bb, ii * 128:(ii + 1) * 128, :]),
                    waits=waits, inc=True)
                xt = XT[:, sl * 1024:(sl + 1) * 1024]
                em("vector", lambda e, xt=xt: e.reduce_sum(STT[:, 0:1], xt, axis=AX.X),
                   waits=[("X", xmil[i])])
                em("vector", lambda e, xt=xt: e.tensor_tensor(U2[:, 0:1024], xt, xt, A.mult))
                em("vector", lambda e: e.reduce_sum(STT[:, 1:2], U2[:, 0:1024], axis=AX.X))
                em("vector", lambda e: e.tensor_scalar(STT[:, 2:3], STT[:, 0:1], 1.0 / D, None, A.mult, A.bypass))
                em("vector", lambda e: e.tensor_scalar(STT[:, 3:4], STT[:, 1:2], 1.0 / D, EPS, A.mult, A.add))
                em("vector", lambda e: e.tensor_tensor(STT[:, 4:5], STT[:, 2:3], STT[:, 2:3], A.mult))
                var_m = em("vector", lambda e: e.tensor_tensor(STT[:, 5:6], STT[:, 3:4], STT[:, 4:5], A.subtract))
                em("scalar", lambda e: e.activation(STT[:, 6:7], STT[:, 5:6], AF.Ln),
                   waits=[("DVE", var_m)])
                rstd_m = em("scalar", lambda e: e.activation(STT[:, 7:8], STT[:, 6:7], AF.Exp, scale=-0.5))
                xnmil[i] = em("vector", lambda e, xt=xt, ii=i: e.tensor_scalar(
                    BIG[:, ii * D:(ii + 1) * D], xt, STT[:, 2:3], STT[:, 7:8], A.subtract, A.mult),
                    waits=[("ACT", rstd_m)])
                twaits = [("DVE", xnmil[i]), ("G", consts_done),
                          ("DVE", xntmil[i - 2] if i >= 2 else 0)]
                if b > 0 and i < 2:
                    twaits.append(("PE", prev["c_pe"]))
                for ss in range(8):
                    trmil[i] = em("tensor", lambda e, ii=i, ss=ss: e.transpose(
                        PSB[:, ((ii % 2) * 8 + ss) * 128:((ii % 2) * 8 + ss + 1) * 128],
                        BIG[:, ii * D + ss * 128: ii * D + (ss + 1) * 128], IDN[:]),
                        waits=twaits if ss == 0 else (), inc=(ss == 7))
                xntmil[i] = em("vector", lambda e, ii=i: e.tensor_copy(
                    XNT[:].rearrange("p (s t) -> p s t", s=8)[:, :, ii * 128:(ii + 1) * 128],
                    PSB[:, (ii % 2) * 1024:((ii % 2) + 1) * 1024].rearrange("p (s t) -> p s t", s=8)),
                    waits=[("PE", trmil[i])])
                for ss in range(8):
                    lgmil[i] = em("tensor", lambda e, ii=i, ss=ss: e.matmul(
                        PSF[:, 1024 + ii * 8:1024 + (ii + 1) * 8],
                        XNT[:, ss * T + ii * 128: ss * T + (ii + 1) * 128],
                        WG[:, ss * 8:(ss + 1) * 8], start=(ss == 0), stop=(ss == 7)),
                        waits=[("DVE", xntmil[i])] if ss == 0 else (), inc=(ss == 7))
                for hh in range(2):
                    m_acc_pe = em("tensor", lambda e, ii=i, hh=hh: e.matmul(
                        PSF[0:1, 1536 + hh * 512:1536 + (hh + 1) * 512], CON1[:],
                        BIG[:, ii * D + hh * 512: ii * D + hh * 512 + 512],
                        start=(ii == 0), stop=(ii == TIL - 1)),
                        inc=(hh == 1 and i == TIL - 1))
            lg_evac = em("vector", lambda e: e.tensor_copy(LGT[:], PSF[:, 1024:1280]),
                         waits=[("PE", lgmil[TIL - 1])])
            mr1_m = em("vector", lambda e: e.tensor_scalar(MR1[:], PSF[0:1, 1536:2560], 1.0 / T, None, A.mult, A.bypass),
                       waits=[("PE", m_acc_pe)])
            for hh in range(2):
                mbp = em("tensor", lambda e, hh=hh: e.matmul(
                    PSF[:, hh * 512:(hh + 1) * 512], CROW[:], MR1[0:1, hh * 512:(hh + 1) * 512],
                    start=True, stop=True),
                    waits=[("DVE", mr1_m)] if hh == 0 else (), inc=(hh == 1))
            mb_m = em("vector", lambda e: e.tensor_copy(GEN[:, MB_C:MB_C + 1024], PSF[:, 0:1024]),
                      waits=[("PE", mbp)])

            if dbg and b == 0:
                em("sync", lambda e: e.dma_start(out=dbg_t["dbg_xn"], in_=BIG[:]), waits=[("DVE", xnmil[TIL - 1])], inc=True)
                em("sync", lambda e: e.dma_start(out=dbg_t["dbg_xnt"], in_=XNT[:]), waits=[("DVE", xntmil[TIL - 1])], inc=True)
                em("sync", lambda e: e.dma_start(out=dbg_t["dbg_lgt"], in_=LGT[:]), waits=[("DVE", lg_evac)], inc=True)

            ctr_cp = [0] * TIL
            ctr_pe = [0] * TIL
            w16_last = 0
            for i in range(TIL):
                em("vector", lambda e, ii=i: e.tensor_tensor(
                    U2[:, 0:1024], BIG[:, ii * D:(ii + 1) * D], GEN[:, MB_C:MB_C + 1024], A.mult),
                    waits=[("DVE", mb_m)])
                u_m = em("vector", lambda e: e.reduce_sum(
                    UWT[:, 0:16], U2[:, 0:1024].rearrange("p (w c) -> p w c", c=64), axis=AX.X))
                em("vector", lambda e, ii=i: e.tensor_tensor(
                    U2[:, 0:1024], BIG[:, ii * D:(ii + 1) * D], BIG[:, ii * D:(ii + 1) * D], A.mult))
                n2_m = em("vector", lambda e: e.reduce_sum(
                    UWT[:, 16:32], U2[:, 0:1024].rearrange("p (w c) -> p w c", c=64), axis=AX.X))
                w_m = em("scalar", lambda e: e.activation(UWT[:, 32:48], UWT[:, 0:16], AF.Exp, scale=-1.0),
                         waits=[("DVE", u_m)])
                w16_last = em("vector", lambda e, ii=i: e.tensor_copy(W16[:, ii * 16:(ii + 1) * 16], UWT[:, 32:48]),
                              waits=[("ACT", w_m)])
                lg = LGT[:, i * 8:(i + 1) * 8]
                em("vector", lambda e, lg=lg: e.reduce_max(UWT[:, 48:49], lg, axis=AX.X))
                em("vector", lambda e, lg=lg: e.tensor_scalar(U2[:, 1024:1032], lg, UWT[:, 48:49], None, A.is_ge, A.bypass))
                em("vector", lambda e, lg=lg: e.scalar_tensor_tensor(U2[:, 1032:1040], U2[:, 1024:1032], -1e30, lg, A.mult, A.add))
                em("vector", lambda e: e.reduce_max(UWT[:, 49:50], U2[:, 1032:1040], axis=AX.X))
                em("vector", lambda e: e.tensor_scalar(U2[:, 1040:1048], U2[:, 1032:1040], UWT[:, 49:50], None, A.is_ge, A.bypass))
                d_m = em("vector", lambda e: e.tensor_tensor(UWT[:, 50:51], UWT[:, 49:50], UWT[:, 48:49], A.subtract))
                eact = em("scalar", lambda e: e.activation(UWT[:, 51:52], UWT[:, 50:51], AF.Exp),
                          waits=[("DVE", d_m)])
                em("vector", lambda e: e.tensor_scalar(UWT[:, 52:53], UWT[:, 51:52], 1.0, None, A.add, A.bypass),
                   waits=[("ACT", eact)])
                em("vector", lambda e: e.reciprocal(UWT[:, 53:54], UWT[:, 52:53]))
                em("vector", lambda e: e.tensor_tensor(UWT[:, 54:55], UWT[:, 51:52], UWT[:, 53:54], A.mult))
                em("vector", lambda e: e.tensor_scalar(U2[:, 1048:1056], U2[:, 1040:1048], UWT[:, 54:55], None, A.mult, A.bypass))
                em("vector", lambda e: e.scalar_tensor_tensor(CMB[:, 0:8], U2[:, 1024:1032], UWT[:, 53:54], U2[:, 1048:1056], A.mult, A.add),
                   waits=[("PE", ctr_pe[i - 1] if i >= 1 else 0)])
                n2c_m = em("vector", lambda e: e.tensor_copy(CMB[:, 32:48], UWT[:, 16:32]))
                ctr_pe[i] = em("tensor", lambda e, ii=i: e.transpose(
                    PSB[0:48, (ii % 2) * 128:(ii % 2) * 128 + 128],
                    CMB[:], IDN[:]),
                    waits=[("DVE", n2c_m), ("DVE", ctr_cp[i - 2] if i >= 2 else 0)], inc=True)
                ctr_cp[i] = em("vector", lambda e, ii=i: e.tensor_copy(
                    GEN[0:48, TR_C + ii * 128:TR_C + (ii + 1) * 128],
                    PSB[0:48, (ii % 2) * 128:(ii % 2) * 128 + 128]),
                    waits=[("PE", ctr_pe[i])])
            em("vector", lambda e: e.reduce_max(GAM[32:48, 4:5], GEN[32:48, TR_C:TR_C + T], axis=AX.X),
               waits=[("DVE", ctr_cp[TIL - 1])])
            gam_m = em("vector", lambda e: e.tensor_scalar(GAM[32:48, 16:17], GAM[32:48, 4:5], -70.0, None, A.add, A.bypass))
            gb_x = em("sync", lambda e: e.dma_start(out=gscr, in_=GAM[32:48, 16:17]),
                      waits=[("DVE", gam_m)], inc=True)
            gb_g = em("gpsimd", lambda e: e.dma_start(out=GRW[:], in_=gscr.rearrange("a b -> (a b)").unsqueeze(0)),
                      waits=[("X", gb_x)], inc=True)
            gbp = em("tensor", lambda e: e.matmul(PSF[:, 0:16], CNR[:], GRW[:], start=True, stop=True),
                     waits=[("G", gb_g), ("DVE", mb_m)], inc=True)
            gb_m = em("vector", lambda e: e.tensor_copy(GB[:], PSF[:, 0:16]),
                      waits=[("PE", gbp)])

            if dbg and b == 0:
                em("sync", lambda e: e.dma_start(out=dbg_t["dbg_tr"], in_=GEN[:, TR_C:TR_C + T]),
                   waits=[("DVE", ctr_cp[TIL - 1])], inc=True)
                em("sync", lambda e: e.dma_start(out=dbg_t["dbg_gb"], in_=GB[:]), waits=[("DVE", gb_m)], inc=True)
                em("sync", lambda e: e.dma_start(out=dbg_t["dbg_w16"], in_=W16[:]), waits=[("DVE", w16_last)], inc=True)

            if "B" not in phases:
                prev = {"out_x": cnt["X"], "d_dve": cnt["DVE"], "c_pe": cnt["PE"], "c_dve": cnt["DVE"]}
                continue

            prev_head_o_last = 0
            prev_head_oevac = mb_m
            attn_store = [0] * H
            for h in range(H):
                s0 = h // 2
                rbase = (h % 2) * 64
                vw_w = [("DVE", w16_last), ("PE", prev_head_o_last), ("X", attn_store[h - 1] if h >= 1 else 0)]
                for j in range(TIL):
                    em("vector", lambda e, j=j, h=h: e.tensor_scalar(
                        GEN[:, VW_C + j * 65:VW_C + j * 65 + 64],
                        BIG[:, j * D + h * HD: j * D + h * HD + 64],
                        W16[:, j * 16 + h: j * 16 + h + 1], None, A.mult, A.bypass),
                        waits=vw_w if j == 0 else ())
                vw_m = em("vector", lambda e, h=h: e.tensor_copy(
                    GEN[:, VW_C:VW_C + 65 * 32].rearrange("p (j c) -> p j c", c=65)[:, :, 64:65],
                    W16[:, 0:512].rearrange("p (j c) -> p j c", c=16)[:, :, h:h + 1]))
                NCB = T // 512
                NJJ = NCB * TIL
                SSL = (0, 512, 1024, 1536, 2560)
                smil = [0] * NJJ
                omil = [0] * NJJ
                expmil = [0] * NJJ
                oevac = [0] * NCB

                def emit_s(jj):
                    ch_, j_ = divmod(jj, TIL)
                    sw = [("ACT", expmil[jj - 5] if jj >= 5 else 0)]
                    if jj == 0:
                        sw += [("DVE", vw_m), ("DVE", gb_m)]
                    smil[jj] = em("tensor", lambda e, s0=s0, rb=rbase, j=j_, ch=ch_, jj=jj: e.matmul(
                        PSF[:, SSL[jj % 5]:SSL[jj % 5] + 512],
                        XNT[rb:rb + 64, s0 * T + j * 128: s0 * T + (j + 1) * 128],
                        XNT[rb:rb + 64, s0 * T + ch * 512: s0 * T + ch * 512 + 512],
                        start=True, stop=True),
                        waits=sw, inc=True)
                for p_ in range(4):
                    emit_s(p_)
                for jj in range(NJJ):
                    ch, j = divmod(jj, TIL)
                    if jj + 4 < NJJ:
                        emit_s(jj + 4)
                    ew = [("PE", smil[jj]),
                          ("PE", omil[jj - 5] if jj >= 5 else prev_head_o_last)]
                    expmil[jj] = em("scalar", lambda e, jj=jj, h=h: e.activation(
                        GEN[:, RS_C + (jj % 5) * 512:RS_C + (jj % 5) * 512 + 512],
                        PSF[:, SSL[jj % 5]:SSL[jj % 5] + 512],
                        AF.Exp, bias=GB[:, h: h + 1]),
                        waits=ew)
                    ow = [("ACT", expmil[jj])]
                    if j == 0:
                        ow.append(("DVE", oevac[ch - 1] if ch >= 1 else prev_head_oevac))
                    omil[jj] = em("tensor", lambda e, j=j, jj=jj: e.matmul(
                        PSF[0:65, 2048:2048 + 512],
                        GEN[:, VW_C + j * 65:VW_C + (j + 1) * 65],
                        GEN[:, RS_C + (jj % 5) * 512:RS_C + (jj % 5) * 512 + 512],
                        start=(j == 0), stop=(j == TIL - 1)),
                        waits=ow, inc=True)
                    if j == TIL - 1:
                        ovw = [("PE", omil[jj])]
                        if ch == 0:
                            ovw.append(("X", attn_store[h - 1] if h >= 1 else 0))
                        oevac[ch] = em("vector", lambda e, ch=ch: e.tensor_copy(
                            GEN[0:65, OST_C + ch * 512:OST_C + (ch + 1) * 512],
                            PSF[0:65, 2048:2048 + 512]),
                            waits=ovw)
                prev_head_o_last = omil[NJJ - 1]
                prev_head_oevac = oevac[NCB - 1]
                attn_store[h] = em("sync", lambda e, h=h: e.dma_start(
                    out=attnT[:, h, :], in_=GEN[0:65, OST_C:OST_C + T]),
                    waits=[("DVE", oevac[NCB - 1])], inc=True)
            b_last_pe = prev_head_o_last
            b_store_x = attn_store[H - 1]

            if "C" not in phases:
                prev = {"out_x": cnt["X"], "d_dve": cnt["DVE"], "c_pe": cnt["PE"], "c_dve": cnt["DVE"]}
                continue

            bp01 = ("DVE", 0)
            bp23 = ("ACT", 0)
            bp45 = ("DVE", prev_head_oevac)
            wi_slot_free = [b_last_pe, b_last_pe]
            wo_slot_free = [b_last_pe, b_last_pe]
            gbc_slot_free = [b_last_pe, b_last_pe]
            gbc_slot_free_d = [0, 0]
            yst = 0
            p2_last_pe = b_last_pe
            p2_last_dve = 0
            for ch in range(NCH):
                for ee in range(E):
                    gw = [(bp45[0], bp45[1]), ("DVE", ctr_cp[TIL - 1]), ("X", b_store_x)]
                    for hh in range(2):
                        gbp = em("tensor", lambda e, ee=ee, ch=ch, hh=hh: e.matmul(
                            PSF[:, 2048 + hh * 512:2048 + hh * 512 + 512],
                            OH[:, ee * 128:(ee + 1) * 128],
                            GEN[0:8, TR_C + ch * CH + hh * 512:TR_C + ch * CH + hh * 512 + 512],
                            start=True, stop=True),
                            waits=gw if hh == 0 else (), inc=(hh == 1))
                    gbc_m = em("vector", lambda e, ee=ee: e.tensor_copy(
                        GEN[:, GBC_C + (ee % 2) * 1024:GBC_C + (ee % 2) * 1024 + 1024], PSF[:, 2048:3072]),
                        waits=[("PE", gbp), ("DVE", gbc_slot_free_d[ee % 2])])
                    bp45 = ("DVE", gbc_m)
                    p1evac_v = 0
                    p1evac_a = 0
                    p1mil = [0] * 32
                    for ht in range(32):
                        wdm = em("gpsimd", lambda e, ee=ee, ht=ht: e.dma_start(
                            out=GEN[:, WI_C + (ht % 2) * 1024:WI_C + (ht % 2) * 1024 + 1024], in_=wih[ee, ht]),
                            waits=[("PE", wi_slot_free[ht % 2]), ("X", b_store_x)], inc=True, key="WI")
                        bp = bp01 if ht % 2 == 0 else bp23
                        mw = [("WI", wdm), (bp[0], bp[1])]
                        for k in range(8):
                            for hh in range(2):
                                p1mil[ht] = em("tensor", lambda e, ht=ht, k=k, hh=hh, ch=ch: e.matmul(
                                    PSF[:, (ht % 2) * 1024 + hh * 512:(ht % 2) * 1024 + hh * 512 + 512],
                                    GEN[:, WI_C + (ht % 2) * 1024 + k * 128:WI_C + (ht % 2) * 1024 + (k + 1) * 128],
                                    XNT[:, k * T + ch * CH + hh * 512: k * T + ch * CH + hh * 512 + 512],
                                    start=(k == 0), stop=(k == 7)),
                                    waits=mw if (k == 0 and hh == 0) else (), inc=(k == 7 and hh == 1))
                        wi_slot_free[ht % 2] = p1mil[ht]
                        evw = [("PE", p1mil[ht])]
                        if ht < 2:
                            evw.append(("PE", p2_last_pe))
                        if ht % 2 == 0:
                            p1evac_v = em("vector", lambda e, ht=ht: e.tensor_copy(
                                BIG[:, ht * 1024:(ht + 1) * 1024], PSF[:, 0:1024]), waits=evw)
                            bp01 = ("DVE", p1evac_v)
                        else:
                            p1evac_a = em("scalar", lambda e, ht=ht: e.copy(
                                BIG[:, ht * 1024:(ht + 1) * 1024], PSF[:, 1024:2048]), waits=evw)
                            bp23 = ("ACT", p1evac_a)
                    for dt in range(8):
                        wdm = em("gpsimd", lambda e, ee=ee, dt=dt: e.dma_start(
                            out=GEN[:, WO_C + (dt % 2) * 4096:WO_C + (dt % 2) * 4096 + 4096], in_=woh[ee, dt]),
                            waits=[("PE", wo_slot_free[dt % 2]), ("X", b_store_x)], inc=True)
                        bp = bp45 if dt % 2 == 0 else bp23
                        mw = [("G", wdm), (bp[0], bp[1]),
                              ("DVE", p1evac_v), ("ACT", p1evac_a)]
                        pcol = 2048 if dt % 2 == 0 else 1024
                        p2m = 0
                        for sth in range(32):
                            for hh in range(2):
                                p2m = em("tensor", lambda e, dt=dt, sth=sth, hh=hh, pcol=pcol: e.matmul(
                                    PSF[:, pcol + hh * 512:pcol + hh * 512 + 512],
                                    GEN[:, WO_C + (dt % 2) * 4096 + sth * 128:WO_C + (dt % 2) * 4096 + (sth + 1) * 128],
                                    BIG[:, sth * 1024 + hh * 512: sth * 1024 + hh * 512 + 512],
                                    start=(sth == 0), stop=(sth == 31)),
                                    waits=mw if (sth == 0 and hh == 0) else (), inc=(sth == 31 and hh == 1))
                        wo_slot_free[dt % 2] = p2m
                        p2_last_pe = p2m
                        ew = [("PE", p2m), ("DVE", gbc_m)]
                        if ee == 0:
                            ew.append(("X", yst))
                        em("vector", lambda e, pcol=pcol, ee=ee: e.tensor_tensor(
                            U2[:, 0:1024], PSF[:, pcol:pcol + 1024],
                            GEN[:, GBC_C + (ee % 2) * 1024:GBC_C + (ee % 2) * 1024 + 1024], A.mult),
                            waits=ew)
                        if ee == 0:
                            p2ev = em("vector", lambda e, dt=dt: e.tensor_copy(
                                YAC[:, dt * 1024:(dt + 1) * 1024], U2[:, 0:1024]))
                        else:
                            p2ev = em("vector", lambda e, dt=dt: e.tensor_tensor(
                                YAC[:, dt * 1024:(dt + 1) * 1024], YAC[:, dt * 1024:(dt + 1) * 1024],
                                U2[:, 0:1024], A.add))
                        if dt % 2 == 0:
                            bp45 = ("DVE", p2ev)
                        else:
                            bp23 = ("DVE", p2ev)
                        p2_last_dve = p2ev
                    gbc_slot_free_d[ee % 2] = p2_last_dve
                yst = em("sync", lambda e, ch=ch: e.dma_start(
                    out=yT[:, ch * CH:(ch + 1) * CH].rearrange("(dt p) t -> p dt t", p=128),
                    in_=YAC[:].rearrange("p (dt t) -> p dt t", t=CH)),
                    waits=[("DVE", p2_last_dve)], inc=True)
            c_last_pe = p2_last_pe
            c_last_dve = p2_last_dve
            c_last_x = yst

            if "D" not in phases:
                prev = {"out_x": cnt["X"], "d_dve": cnt["DVE"], "c_pe": cnt["PE"], "c_dve": cnt["DVE"]}
                continue

            attnT_r = attnT
            ain_mil = [0, 0]
            yin_mil = [0, 0]
            ain_free = [c_last_pe, c_last_pe]
            yin_free = [c_last_pe, c_last_pe]
            acc_free = [0, 0]
            xt_free = [c_last_dve, c_last_dve]
            d_dve = c_last_dve
            out_x = c_last_x
            atr_last = [0, 0]
            ytr_last = [0, 0]
            yadd_last = [0, 0]
            AIN_CS = (AIN_C0, AIN_C1)
            YIN_CS = (YIN_C0, YIN_C1)
            for pp in range(16):
                ps = pp % 2
                ain_mil[ps] = em("sync", lambda e, pp=pp, ps=ps: e.dma_start(
                    out=GEN[0:65, AIN_CS[ps]:AIN_CS[ps] + 4096].rearrange("r (h t) -> r h t", t=256),
                    in_=attnT_r[:, :, pp * 256:(pp + 1) * 256]),
                    waits=[("PE", ain_free[ps]), ("X", c_last_x), ("DVE", c_last_dve)], inc=True)
                yin_mil[ps] = em("sync", lambda e, pp=pp, ps=ps: e.dma_start(
                    out=GEN[:, YIN_CS[ps]:YIN_CS[ps] + 2048].rearrange("p (s t) -> p s t", t=256),
                    in_=yT[:, pp * 256:(pp + 1) * 256].rearrange("(s p) t -> p s t", p=128)),
                    waits=[("PE", yin_free[ps]), ("X", c_last_x), ("DVE", c_last_dve)], inc=True)
                for t2 in range(2):
                    i = pp * 2 + t2
                    isl = i % 2
                    xm = em("sync", lambda e, bb=b, ii=i, isl=isl: e.dma_start(
                        out=XT[:, isl * 1024:(isl + 1) * 1024], in_=x[bb, ii * 128:(ii + 1) * 128, :]),
                        waits=[("DVE", xt_free[isl])], inc=True)
                    tw = [("X", xm), ("DVE", yadd_last[1 - isl] if i >= 1 else 0),
                          ("DVE", d_dve if i == 0 else 0)]
                    atr = 0
                    for h in range(H):
                        atr = em("tensor", lambda e, h=h, ps=ps, t2=t2: e.transpose(
                            PSB[:, h * 128:h * 128 + 65],
                            GEN[0:65, AIN_CS[ps] + h * 256 + t2 * 128:AIN_CS[ps] + h * 256 + t2 * 128 + 128],
                            IDN[0:65, 0:65]),
                            waits=tw if h == 0 else (), inc=(h == H - 1))
                    atr_last[isl] = atr
                    ain_free[ps] = atr
                    for h in range(H):
                        em("vector", lambda e, h=h: e.reciprocal(RCP[:, h:h + 1], PSB[:, h * 128 + 64:h * 128 + 65]),
                           waits=[("PE", atr)] if h == 0 else ())
                    st_m = 0
                    for h in range(H):
                        st_m = em("vector", lambda e, h=h, isl=isl: e.scalar_tensor_tensor(
                            U2[:, isl * 1024 + h * 64: isl * 1024 + (h + 1) * 64],
                            PSB[:, h * 128:h * 128 + 64], RCP[:, h:h + 1],
                            XT[:, isl * 1024 + h * 64: isl * 1024 + (h + 1) * 64], A.mult, A.add),
                            waits=[("X", xm), ("SO", acc_free[isl])] if h == 0 else ())
                    xt_free[isl] = st_m
                    ytr = 0
                    for sth in range(8):
                        ytr = em("tensor", lambda e, sth=sth, ps=ps, t2=t2: e.transpose(
                            PSB[:, sth * 128:(sth + 1) * 128],
                            GEN[:, YIN_CS[ps] + sth * 256 + t2 * 128:YIN_CS[ps] + sth * 256 + t2 * 128 + 128],
                            IDN[:]),
                            waits=[("DVE", st_m), ("X", xm)] if sth == 0 else (), inc=(sth == 7))
                    ytr_last[isl] = ytr
                    yin_free[ps] = ytr
                    ya = 0
                    for sth in range(8):
                        ya = em("vector", lambda e, sth=sth, isl=isl: e.tensor_tensor(
                            U2[:, isl * 1024 + sth * 128: isl * 1024 + (sth + 1) * 128],
                            U2[:, isl * 1024 + sth * 128: isl * 1024 + (sth + 1) * 128],
                            PSB[:, sth * 128:(sth + 1) * 128], A.add),
                            waits=[("PE", ytr)] if sth == 0 else ())
                    yadd_last[isl] = ya
                    d_dve = ya
                    acc_free[isl] = em("sync", lambda e, bb=b, ii=i, isl=isl: e.dma_start(
                        out=out[bb, ii * 128:(ii + 1) * 128, :], in_=U2[:, isl * 1024:(isl + 1) * 1024]),
                        waits=[("DVE", ya)], inc=True, key="SO")
                    out_x = acc_free[isl]
            prev = {"out_x": out_x, "d_dve": d_dve, "c_pe": cnt["PE"], "c_dve": cnt["DVE"]}

        em("sync", lambda e: e.nop(), waits=[("X", cnt["X"]), ("SO", cnt["SO"])])
        em("gpsimd", lambda e: e.nop(), waits=[("G", cnt["G"]), ("WI", cnt["WI"])])

        @block.sync
        def _(eng):
            for f in plan["sync"]:
                f(eng, sems)

        @block.gpsimd
        def _(eng):
            for f in plan["gpsimd"]:
                f(eng, sems)

        @block.tensor
        def _(eng):
            for f in plan["tensor"]:
                f(eng, sems)

        @block.scalar
        def _(eng):
            for f in plan["scalar"]:
                f(eng, sems)

        @block.vector
        def _(eng):
            for f in plan["vector"]:
                f(eng, sems)

    return nc


def host_inputs(L0_attn_x, L0_ffn_w_gate, L0_ffn_w_in, L0_ffn_w_out):
    xf = np.ascontiguousarray(np.asarray(L0_attn_x, dtype=np.float32))
    wg = np.asarray(L0_ffn_w_gate, dtype=np.float32)
    wi = np.asarray(L0_ffn_w_in, dtype=np.float32)
    wo = np.asarray(L0_ffn_w_out, dtype=np.float32)
    WIH = np.ascontiguousarray(wi.reshape(E, 32, 128, 8, 128).transpose(0, 1, 4, 3, 2)).reshape(E, 32, 128, 1024)
    WOH = np.ascontiguousarray(wo.reshape(E, 8, 128, 32, 128).transpose(0, 1, 4, 3, 2)).reshape(E, 8, 128, 4096)
    WGH = np.ascontiguousarray(wg.reshape(8, 8, 128).transpose(2, 1, 0)).reshape(128, 64)
    ident = np.eye(128, dtype=np.float32)
    oh = np.zeros((8, 1024), dtype=np.float32)
    for e in range(8):
        oh[e, e * 128:(e + 1) * 128] = 1.0
    cones = np.ones((128, 1), dtype=np.float32)
    crow = np.ones((1, 128), dtype=np.float32)
    cnrow = -np.ones((1, 128), dtype=np.float32)
    shards = [np.ascontiguousarray(xf[2 * c:2 * c + 2]) for c in range(8)]
    in_maps = [{"x": shards[c], "wih": WIH, "woh": WOH, "wgh": WGH, "ident": ident,
                "oh": oh, "cones": cones, "crow": crow, "cnrow": cnrow} for c in range(8)]
    return in_maps


def kernel(L0_attn_x, L0_ffn_w_gate, L0_ffn_w_in, L0_ffn_w_out):
    in_maps = host_inputs(L0_attn_x, L0_ffn_w_gate, L0_ffn_w_in, L0_ffn_w_out)
    nc = build_program()
    res = run_bass_kernel_spmd(nc, in_maps, core_ids=list(range(8)))
    return np.concatenate([np.asarray(r["out"]) for r in res.results], axis=0)


# version 10
# speedup vs baseline: 1.4238x; 1.2528x over previous
import numpy as np
import concourse.bass as bass
import concourse.mybir as mybir
from concourse.alu_op_type import AluOpType
from concourse.bass_utils import run_bass_kernel_spmd
from concourse.bass_core_rust import IndirectOffsetOnAxis

F32 = mybir.dt.float32
BF16 = mybir.dt.bfloat16
I32 = mybir.dt.int32
AF = mybir.ActivationFunctionType
AX = mybir.AxisListType
A = AluOpType

NB = 2
T = 4096
D = 1024
H = 16
HD = 64
E = 8
TIL = 32
CH = 1024
NCH = T // CH
EPS = 1e-5
CAP = 2048
NWC = 2
GT = CAP // 128
XW = 1032
OOB = float(1 << 20)

MB_C = 0
OST_C = 0
VW_C = 4096
RS_C = 6176
XG_C = 0
WI_C = 4128
WO_C = 6176
YTS_C = 14368
AIN_C0 = 0
AIN_C1 = 4096
YIN_C0 = 8192
YIN_C1 = 9216
TR_C = 12320
GEN_W = TR_C + T

ENGS = ("sync", "gpsimd", "tensor", "scalar", "vector")
EK = {"sync": "X", "gpsimd": "G", "tensor": "PE", "scalar": "ACT", "vector": "DVE"}
DMAK = ("X", "G", "WI", "WO", "GA", "SC", "IX", "XN", "YZ", "SO")


def build_program(phases="ABCD", dbg=False, skip=()):
    nc = bass.Bass("TRN2", target_bir_lowering=False, debug=False)
    x = nc.dram_tensor("x", [NB, T, D], F32, kind="ExternalInput").ap()
    wih = nc.dram_tensor("wih", [E, 32, 128, 1024], F32, kind="ExternalInput").ap()
    woh = nc.dram_tensor("woh", [E, 8, 128, 4096], F32, kind="ExternalInput").ap()
    wgh = nc.dram_tensor("wgh", [128, 64], F32, kind="ExternalInput").ap()
    ident_in = nc.dram_tensor("ident", [128, 128], F32, kind="ExternalInput").ap()
    cones_in = nc.dram_tensor("cones", [128, 1], F32, kind="ExternalInput").ap()
    crow_in = nc.dram_tensor("crow", [1, 128], F32, kind="ExternalInput").ap()
    cnrow_in = nc.dram_tensor("cnrow", [1, 128], F32, kind="ExternalInput").ap()
    ide8_in = nc.dram_tensor("ide8", [40, 8], F32, kind="ExternalInput").ap()
    crowf_in = nc.dram_tensor("crowf", [1, 128], F32, kind="ExternalInput").ap()
    out = nc.dram_tensor("out", [NB, T, D], F32, kind="ExternalOutput").ap()
    okind = "ExternalOutput" if dbg else "Internal"
    attnT = nc.dram_tensor("attnT", [65, H, T], BF16, kind=okind).ap()
    ydr = nc.dram_tensor("ydr", [T, D], BF16, kind=okind).ap()
    xnd = nc.dram_tensor("xnd", [T, XW], BF16, kind="Internal").ap()
    idxd = nc.dram_tensor("idxd", [128 * E * GT, 1], I32, kind=okind).ap()
    gscr = nc.dram_tensor("gscr", [16, 1], F32, kind="Internal").ap()
    idxd_flat = idxd
    idxd2d = idxd.rearrange("(p c) o -> p (c o)", p=128)
    dbg_t = {}
    if dbg:
        for nm, shp, dt in (("dbg_xn", [128, TIL * D], BF16), ("dbg_xnt", [128, 8 * T], BF16),
                            ("dbg_lgt", [128, 256], F32), ("dbg_tr", [128, T], BF16),
                            ("dbg_gb", [128, 16], F32), ("dbg_w16", [128, 512], F32),
                            ("dbg_lgt1", [128, 256], F32)):
            dbg_t[nm] = nc.dram_tensor(nm, shp, dt, kind="ExternalOutput").ap()

    plan = {e: [] for e in ENGS}
    cnt = {k: 0 for k in DMAK + ("PE", "ACT", "DVE")}
    last = {"ACT": 0, "DVE": 0}

    def em(eng, fn, waits=(), inc=False, key=None, chain=True):
        key = key or EK[eng]
        amt = 16 if key in DMAK else 1
        wl = list(waits)
        if key in ("ACT", "DVE"):
            inc = True
            if chain and last[key] > 0:
                wl.append((key, last[key]))
        mil = None
        if inc:
            cnt[key] += amt
            mil = cnt[key]
            if key in ("ACT", "DVE"):
                last[key] = mil
        wd = {}
        for k, v in wl:
            if v > 0:
                wd[k] = max(wd.get(k, 0), v)
        wt = tuple(wd.items())

        def run(e, s, fn=fn, wt=wt, inc=inc, key=key, amt=amt):
            for k, v in wt:
                e.wait_ge(s[k], v)
            instr = fn(e)
            if inc:
                instr.then_inc(s[key], amt)
        plan[eng].append(run)
        return mil

    import contextlib
    st_ = contextlib.ExitStack()
    with st_:
        def _sb(name, shape, dt):
            return st_.enter_context(nc.sbuf_tensor(name, shape, dt))
        XNT = _sb("XNT", [128, 8 * T], BF16)
        BIG = _sb("BIG", [128, TIL * D], BF16)
        GEN = _sb("GEN", [128, GEN_W], BF16)
        U2 = _sb("U2", [128, 2048], F32)
        XT = _sb("XT", [128, 2048], F32)
        YAC = _sb("YAC", [128, 8 * CH], BF16)
        ZRO = _sb("ZRO", [128, 1024], BF16)
        W16 = _sb("W16", [128, 512], F32)
        GTS = _sb("GTS", [128, 256], BF16)
        CMB = _sb("CMB", [128, 48], BF16)
        LGT = _sb("LGT", [128, 256], F32)
        STT = _sb("STT", [128, 16], F32)
        UWT = _sb("UWT", [128, 64], F32)
        MR1 = _sb("MR1", [1, 1024], BF16)
        GAM = _sb("GAM", [128, 24], F32)
        GB = _sb("GB", [128, 16], F32)
        GRW = _sb("GRW", [1, 16], BF16)
        RCP = _sb("RCP", [128, 16], F32)
        IDN = _sb("IDN", [128, 128], BF16)
        IDN32 = _sb("IDN32", [128, 128], F32)
        IDE8 = _sb("IDE8", [40, 8], F32)
        CROWF = _sb("CROWF", [1, 128], F32)
        WG = _sb("WG", [128, 64], BF16)
        CON1 = _sb("CON1", [128, 1], BF16)
        CROW = _sb("CROW", [1, 128], BF16)
        CNR = _sb("CNR", [1, 128], BF16)
        IOR = _sb("IOR", [128, 8], F32)
        IORI = _sb("IORI", [128, 8], I32)
        TOKID = _sb("TOKID", [128, 32], I32)
        OOBT = _sb("OOBT", [128, E * GT], I32)
        ISL = _sb("ISL", [128, 8], I32)
        IDXS = _sb("IDXS", [128, 32], I32)
        PSF = st_.enter_context(nc.psum_tensor("PSF", [128, 3072], F32))
        PSB = st_.enter_context(nc.psum_tensor("PSB", [128, 2048], BF16))
        semh = {}
        for k in DMAK + ("PE", "ACT", "DVE"):
            semh[k] = st_.enter_context(nc.semaphore("S" + k))
        block = st_.enter_context(nc.Block())
        sems = semh

        BREG = {}

        def _mkregs(e):
            BREG["t"] = e.to_reg(T - 1)
            BREG["s"] = e.to_reg(128 * E * GT - 1)
        em("gpsimd", _mkregs)

        for dst, src in ((IDN, ident_in), (WG, wgh), (CON1, cones_in),
                         (CROW, crow_in), (CNR, cnrow_in), (IDN32, ident_in), (IDE8, ide8_in), (CROWF, crowf_in)):
            consts_done = em("gpsimd", lambda e, d=dst, sc=src: e.dma_start(out=d[:], in_=sc), inc=True)
        em("gpsimd", lambda e: e.iota(IORI[:], pattern=[[1, 8]], base=0, channel_multiplier=0,
                                      allow_small_or_imprecise_dtypes=True), inc=True, key="G")
        em("gpsimd", lambda e: e.iota(TOKID[:], pattern=[[128, 32]], base=0, channel_multiplier=1), inc=True, key="G")
        iota_done = em("gpsimd", lambda e: e.iota(OOBT[:], pattern=[[0, E * GT]], base=1 << 20,
                                                  channel_multiplier=0), inc=True, key="G")
        ior_m = em("vector", lambda e: e.tensor_copy(IOR[:], IORI[:]), waits=[("G", iota_done)])
        zro_m = em("vector", lambda e: e.memset(ZRO[:], 0.0))
        pre_mil = em("sync", lambda e: e.dma_start(out=idxd2d, in_=OOBT[:]),
                     waits=[("G", iota_done)], inc=True)

        prev = {"out_so": 0, "d_dve": 0, "c_pe": 0, "x_all": 0, "sc_fin": 0}

        for b in range(NB):
            xmil = [0] * TIL
            xnmil = [0] * TIL
            xntmil = [0] * TIL
            trmil = [0] * TIL
            lgmil = [0] * TIL
            for i in range(TIL):
                sl = i % 2
                waits = [("DVE", xnmil[i - 2] if i >= 2 else 0)]
                if b > 0 and i < 2:
                    waits += [("SO", prev["out_so"]), ("DVE", prev["d_dve"])]
                xmil[i] = em("sync", lambda e, sl=sl, bb=b, ii=i: e.dma_start(
                    out=XT[:, sl * 1024:(sl + 1) * 1024], in_=x[bb, ii * 128:(ii + 1) * 128, :]),
                    waits=waits, inc=True)
                xt = XT[:, sl * 1024:(sl + 1) * 1024]
                em("vector", lambda e, xt=xt: e.reduce_sum(STT[:, 0:1], xt, axis=AX.X),
                   waits=[("X", xmil[i])])
                em("vector", lambda e, xt=xt: e.tensor_tensor(U2[:, 0:1024], xt, xt, A.mult))
                em("vector", lambda e: e.reduce_sum(STT[:, 1:2], U2[:, 0:1024], axis=AX.X))
                em("vector", lambda e: e.tensor_scalar(STT[:, 2:3], STT[:, 0:1], 1.0 / D, None, A.mult, A.bypass))
                em("vector", lambda e: e.tensor_scalar(STT[:, 3:4], STT[:, 1:2], 1.0 / D, EPS, A.mult, A.add))
                em("vector", lambda e: e.tensor_tensor(STT[:, 4:5], STT[:, 2:3], STT[:, 2:3], A.mult))
                var_m = em("vector", lambda e: e.tensor_tensor(STT[:, 5:6], STT[:, 3:4], STT[:, 4:5], A.subtract))
                em("scalar", lambda e: e.activation(STT[:, 6:7], STT[:, 5:6], AF.Ln),
                   waits=[("DVE", var_m)])
                rstd_m = em("scalar", lambda e: e.activation(STT[:, 7:8], STT[:, 6:7], AF.Exp, scale=-0.5))
                xnmil[i] = em("vector", lambda e, xt=xt, ii=i: e.tensor_scalar(
                    BIG[:, ii * D:(ii + 1) * D], xt, STT[:, 2:3], STT[:, 7:8], A.subtract, A.mult),
                    waits=[("ACT", rstd_m)])
                xnw = [("DVE", xnmil[i])]
                if b > 0 and i < 2:
                    xnw.append(("GA", prev["ga_fin"]))
                xn_st = em("gpsimd", lambda e, bb=b, ii=i: e.dma_start(
                    out=xnd[ii * 128:(ii + 1) * 128, 0:1024], in_=BIG[:, ii * D:(ii + 1) * D]),
                    waits=xnw, inc=True, key="XN")
                yzw = [("DVE", zro_m)]
                if b > 0 and i < 2:
                    yzw.append(("X", prev["x_all"]))
                em("gpsimd", lambda e, ii=i: e.dma_start(
                    out=ydr[ii * 128:(ii + 1) * 128, :], in_=ZRO[:]),
                    waits=yzw, inc=True, key="YZ")
                twaits = [("DVE", xnmil[i]), ("G", consts_done),
                          ("DVE", xntmil[i - 2] if i >= 2 else 0)]
                if b > 0 and i < 2:
                    twaits.append(("PE", prev["c_pe"]))
                for ss in range(8):
                    trmil[i] = em("tensor", lambda e, ii=i, ss=ss: e.transpose(
                        PSB[:, ((ii % 2) * 8 + ss) * 128:((ii % 2) * 8 + ss + 1) * 128],
                        BIG[:, ii * D + ss * 128: ii * D + (ss + 1) * 128], IDN[:]),
                        waits=twaits if ss == 0 else (), inc=(ss == 7))
                xntmil[i] = em("vector", lambda e, ii=i: e.tensor_copy(
                    XNT[:].rearrange("p (s t) -> p s t", s=8)[:, :, ii * 128:(ii + 1) * 128],
                    PSB[:, (ii % 2) * 1024:((ii % 2) + 1) * 1024].rearrange("p (s t) -> p s t", s=8)),
                    waits=[("PE", trmil[i])])
                for ss in range(8):
                    lgmil[i] = em("tensor", lambda e, ii=i, ss=ss: e.matmul(
                        PSF[:, 1024 + ii * 8:1024 + (ii + 1) * 8],
                        XNT[:, ss * T + ii * 128: ss * T + (ii + 1) * 128],
                        WG[:, ss * 8:(ss + 1) * 8], start=(ss == 0), stop=(ss == 7)),
                        waits=[("DVE", xntmil[i])] if ss == 0 else (), inc=(ss == 7))
                for hh in range(2):
                    m_acc_pe = em("tensor", lambda e, ii=i, hh=hh: e.matmul(
                        PSF[0:1, 1536 + hh * 512:1536 + (hh + 1) * 512], CON1[:],
                        BIG[:, ii * D + hh * 512: ii * D + hh * 512 + 512],
                        start=(ii == 0), stop=(ii == TIL - 1)),
                        inc=(hh == 1 and i == TIL - 1))
            xnd_all = cnt["XN"]
            yz_all = cnt["YZ"]
            lg_evac = em("vector", lambda e: e.tensor_copy(LGT[:], PSF[:, 1024:1280]),
                         waits=[("PE", lgmil[TIL - 1])])
            mr1_m = em("vector", lambda e: e.tensor_scalar(MR1[:], PSF[0:1, 1536:2560], 1.0 / T, None, A.mult, A.bypass),
                       waits=[("PE", m_acc_pe)])
            for hh in range(2):
                mbp = em("tensor", lambda e, hh=hh: e.matmul(
                    PSF[:, hh * 512:(hh + 1) * 512], CROW[:], MR1[0:1, hh * 512:(hh + 1) * 512],
                    start=True, stop=True),
                    waits=[("DVE", mr1_m)] if hh == 0 else (), inc=(hh == 1))
            mb_m = em("vector", lambda e: e.tensor_copy(GEN[:, MB_C:MB_C + 1024], PSF[:, 0:1024]),
                      waits=[("PE", mbp)])

            if dbg and b == 1:
                em("sync", lambda e: e.dma_start(out=dbg_t["dbg_lgt1"], in_=LGT[:]), waits=[("DVE", lg_evac)], inc=True)
            if dbg and b == 0:
                em("sync", lambda e: e.dma_start(out=dbg_t["dbg_xn"], in_=BIG[:]), waits=[("DVE", xnmil[TIL - 1])], inc=True)
                em("sync", lambda e: e.dma_start(out=dbg_t["dbg_xnt"], in_=XNT[:]), waits=[("DVE", xntmil[TIL - 1])], inc=True)
                em("sync", lambda e: e.dma_start(out=dbg_t["dbg_lgt"], in_=LGT[:]), waits=[("DVE", lg_evac)], inc=True)

            ctr_cp = [0] * TIL
            ctr_pe = [0] * TIL
            w16_last = 0
            gts_st = [0] * TIL
            for i in range(TIL):
                em("vector", lambda e, ii=i: e.tensor_tensor(
                    U2[:, 0:1024], BIG[:, ii * D:(ii + 1) * D], GEN[:, MB_C:MB_C + 1024], A.mult),
                    waits=[("DVE", mb_m)])
                u_m = em("vector", lambda e: e.reduce_sum(
                    UWT[:, 0:16], U2[:, 0:1024].rearrange("p (w c) -> p w c", c=64), axis=AX.X))
                em("vector", lambda e, ii=i: e.tensor_tensor(
                    U2[:, 0:1024], BIG[:, ii * D:(ii + 1) * D], BIG[:, ii * D:(ii + 1) * D], A.mult))
                n2_m = em("vector", lambda e: e.reduce_sum(
                    UWT[:, 16:32], U2[:, 0:1024].rearrange("p (w c) -> p w c", c=64), axis=AX.X))
                w_m = em("scalar", lambda e: e.activation(UWT[:, 32:48], UWT[:, 0:16], AF.Exp, scale=-1.0),
                         waits=[("DVE", u_m)])
                w16_last = em("vector", lambda e, ii=i: e.tensor_copy(W16[:, ii * 16:(ii + 1) * 16], UWT[:, 32:48]),
                              waits=[("ACT", w_m)])
                lg = LGT[:, i * 8:(i + 1) * 8]
                em("vector", lambda e, lg=lg: e.reduce_max(UWT[:, 48:49], lg, axis=AX.X))
                em("vector", lambda e, lg=lg: e.tensor_scalar(U2[:, 1024:1032], lg, UWT[:, 48:49], None, A.is_ge, A.bypass))
                em("vector", lambda e, lg=lg: e.scalar_tensor_tensor(U2[:, 1032:1040], U2[:, 1024:1032], -1e30, lg, A.mult, A.add))
                em("vector", lambda e: e.reduce_max(UWT[:, 49:50], U2[:, 1032:1040], axis=AX.X))
                em("vector", lambda e: e.tensor_scalar(U2[:, 1040:1048], U2[:, 1032:1040], UWT[:, 49:50], None, A.is_ge, A.bypass))
                d_m = em("vector", lambda e: e.tensor_tensor(UWT[:, 50:51], UWT[:, 49:50], UWT[:, 48:49], A.subtract))
                eact = em("scalar", lambda e: e.activation(UWT[:, 51:52], UWT[:, 50:51], AF.Exp),
                          waits=[("DVE", d_m)])
                em("vector", lambda e: e.tensor_scalar(UWT[:, 52:53], UWT[:, 51:52], 1.0, None, A.add, A.bypass),
                   waits=[("ACT", eact)])
                em("vector", lambda e: e.reciprocal(UWT[:, 53:54], UWT[:, 52:53]))
                em("vector", lambda e: e.tensor_tensor(UWT[:, 54:55], UWT[:, 51:52], UWT[:, 53:54], A.mult))
                em("vector", lambda e: e.tensor_scalar(U2[:, 1048:1056], U2[:, 1040:1048], UWT[:, 54:55], None, A.mult, A.bypass))
                em("vector", lambda e: e.scalar_tensor_tensor(CMB[:, 0:8], U2[:, 1024:1032], UWT[:, 53:54], U2[:, 1048:1056], A.mult, A.add),
                   waits=[("PE", ctr_pe[i - 1] if i >= 1 else 0)])
                gts_m = em("vector", lambda e, ii=i: e.tensor_copy(GTS[:, ii * 8:(ii + 1) * 8], CMB[:, 0:8]))
                gts_st[i] = em("gpsimd", lambda e, ii=i: e.dma_start(
                    out=xnd[ii * 128:(ii + 1) * 128, 1024:1032], in_=GTS[:, ii * 8:(ii + 1) * 8]),
                    waits=[("DVE", gts_m)], inc=True, key="XN")
                n2c_m = em("vector", lambda e: e.tensor_copy(CMB[:, 32:48], UWT[:, 16:32]))
                ctr_pe[i] = em("tensor", lambda e, ii=i: e.transpose(
                    PSB[0:48, (ii % 2) * 128:(ii % 2) * 128 + 128],
                    CMB[:], IDN[:]),
                    waits=[("DVE", n2c_m), ("DVE", ctr_cp[i - 2] if i >= 2 else 0)], inc=True)
                ctr_cp[i] = em("vector", lambda e, ii=i: e.tensor_copy(
                    GEN[0:48, TR_C + ii * 128:TR_C + (ii + 1) * 128],
                    PSB[0:48, (ii % 2) * 128:(ii % 2) * 128 + 128]),
                    waits=[("PE", ctr_pe[i])])
            xnd_all = cnt["XN"]
            em("vector", lambda e: e.reduce_max(GAM[32:48, 4:5], GEN[32:48, TR_C:TR_C + T], axis=AX.X),
               waits=[("DVE", ctr_cp[TIL - 1])])
            gam_m = em("vector", lambda e: e.tensor_scalar(GAM[32:48, 16:17], GAM[32:48, 4:5], -70.0, None, A.add, A.bypass))
            gb_x = em("sync", lambda e: e.dma_start(out=gscr, in_=GAM[32:48, 16:17]),
                      waits=[("DVE", gam_m)], inc=True)
            gb_g = em("gpsimd", lambda e: e.dma_start(out=GRW[:], in_=gscr.rearrange("a b -> (a b)").unsqueeze(0)),
                      waits=[("X", gb_x)], inc=True)
            gbp = em("tensor", lambda e: e.matmul(PSF[:, 0:16], CNR[:], GRW[:], start=True, stop=True),
                     waits=[("G", gb_g), ("DVE", mb_m)], inc=True)
            gb_m = em("vector", lambda e: e.tensor_copy(GB[:], PSF[:, 0:16]),
                      waits=[("PE", gbp)])

            if dbg and b == 0:
                em("sync", lambda e: e.dma_start(out=dbg_t["dbg_tr"], in_=GEN[:, TR_C:TR_C + T]),
                   waits=[("DVE", ctr_cp[TIL - 1])], inc=True)
                em("sync", lambda e: e.dma_start(out=dbg_t["dbg_gb"], in_=GB[:]), waits=[("DVE", gb_m)], inc=True)
                em("sync", lambda e: e.dma_start(out=dbg_t["dbg_w16"], in_=W16[:]), waits=[("DVE", w16_last)], inc=True)

            em("vector", lambda e: e.tensor_scalar(U2[0:8, 0:2048], GEN[0:8, TR_C:TR_C + 2048], 0.0, None, A.is_gt, A.bypass),
               waits=[("DVE", ctr_cp[TIL - 1])])
            em("vector", lambda e: e.tensor_scalar(U2[32:40, 0:2048], GEN[0:8, TR_C + 2048:TR_C + T], 0.0, None, A.is_gt, A.bypass))
            scan_m = em("vector", lambda e: e.tensor_tensor_scan(
                XT[0:40, 0:2048], U2[0:40, 0:2048], U2[0:40, 0:2048], 0.0, A.add, A.bypass))
            tac = em("tensor", lambda e: e.transpose(PSF[0:1, 1040:1048], XT[0:8, 2047:2048], IDE8[0:8, :]),
                     waits=[("DVE", scan_m)], inc=True)
            tacp = em("vector", lambda e: e.tensor_copy(GAM[0:1, 16:24], PSF[0:1, 1040:1048]),
                      waits=[("PE", tac)])
            tb = em("tensor", lambda e: e.matmul(PSF[:, 1048:1056], CROWF[:], GAM[0:1, 16:24], start=True, stop=True),
                    waits=[("DVE", tacp)], inc=True)
            tb_m = em("vector", lambda e: e.tensor_copy(GAM[:, 8:16], PSF[:, 1048:1056]),
                      waits=[("PE", tb)])
            a3_sc = [0] * TIL
            a3_pt = [0] * TIL
            for i in range(TIL):
                src = (XT[0:8, i * 128:(i + 1) * 128] if i < 16
                       else XT[32:40, (i - 16) * 128:(i - 15) * 128])
                idap = IDE8[0:8, :] if i < 16 else IDE8[32:40, :]
                a3t = em("tensor", lambda e, src=src, idap=idap, ii=i: e.transpose(
                    PSF[0:128, 1024 + (ii % 2) * 8:1024 + (ii % 2) * 8 + 8], src, idap),
                    waits=[("DVE", a3_pt[i - 2] if i >= 2 else scan_m),
                           ("G", iota_done)], inc=True)
                a3_pt[i] = em("vector", lambda e, ii=i: e.tensor_copy(UWT[:, 0:8], PSF[0:128, 1024 + (ii % 2) * 8:1024 + (ii % 2) * 8 + 8]),
                   waits=[("PE", a3t)])
                if i >= 16:
                    a3_pt[i] = em("vector", lambda e: e.tensor_tensor(UWT[:, 0:8], UWT[:, 0:8], GAM[:, 8:16], A.add),
                                  waits=[("DVE", tb_m)])
                gts = GTS[:, i * 8:(i + 1) * 8]
                em("vector", lambda e, gts=gts: e.scalar_tensor_tensor(
                    U2[:, 1024:1032], IOR[:], -1e-6, gts, A.mult, A.add))
                em("vector", lambda e: e.reduce_max(UWT[:, 8:9], U2[:, 1024:1032], axis=AX.X))
                em("vector", lambda e: e.tensor_scalar(UWT[:, 16:24], U2[:, 1024:1032], UWT[:, 8:9], None, A.is_ge, A.bypass))
                em("vector", lambda e: e.tensor_scalar(UWT[:, 24:32], U2[:, 1024:1032], 0.0, None, A.is_gt, A.bypass))
                em("vector", lambda e: e.tensor_tensor(UWT[:, 32:40], UWT[:, 24:32], UWT[:, 16:24], A.subtract))
                em("vector", lambda e: e.reduce_sum(UWT[:, 9:10], UWT[:, 16:24], axis=AX.X))
                em("vector", lambda e: e.reduce_sum(UWT[:, 10:11], UWT[:, 32:40], axis=AX.X))
                for r_, mc, cc_ in ((0, 16, 9), (1, 32, 10)):
                    base = 40 + r_ * 8
                    em("vector", lambda e, mc=mc: e.tensor_tensor(UWT[:, 56:64], UWT[:, mc:mc + 8], IOR[:], A.mult))
                    em("vector", lambda e, base=base: e.reduce_sum(UWT[:, base:base + 1], UWT[:, 56:64], axis=AX.X))
                    em("vector", lambda e, mc=mc: e.tensor_tensor(UWT[:, 56:64], UWT[:, mc:mc + 8], UWT[:, 0:8], A.mult))
                    em("vector", lambda e, base=base: e.reduce_sum(UWT[:, base + 1:base + 2], UWT[:, 56:64], axis=AX.X))
                    em("vector", lambda e, base=base: e.scalar_tensor_tensor(
                        UWT[:, base + 2:base + 3], UWT[:, base:base + 1], float(CAP), UWT[:, base + 1:base + 2], A.mult, A.add))
                    em("vector", lambda e, base=base: e.tensor_scalar(
                        UWT[:, base + 2:base + 3], UWT[:, base + 2:base + 3], 1.0, None, A.subtract, A.bypass))
                    em("vector", lambda e, cc_=cc_: e.tensor_scalar(UWT[:, 56:57], UWT[:, cc_:cc_ + 1], 1.0, None, A.not_equal, A.bypass))
                    em("vector", lambda e, base=base: e.tensor_scalar(UWT[:, 57:58], UWT[:, base + 1:base + 2], CAP + 0.5, None, A.is_gt, A.bypass))
                    em("vector", lambda e: e.tensor_tensor(UWT[:, 58:59], UWT[:, 56:57], UWT[:, 57:58], A.add))
                    em("vector", lambda e, base=base: e.tensor_copy(ISL[:, 2:3], UWT[:, base + 2:base + 3]),
                       waits=[("SC", a3_sc[i - 1] if i >= 1 else 0)])
                    em("vector", lambda e: e.tensor_scalar(ISL[:, 3:4], ISL[:, 2:3], 127, None, A.bitwise_and, A.bypass))
                    em("vector", lambda e: e.tensor_scalar(ISL[:, 4:5], ISL[:, 2:3], 7, None, A.arith_shift_right, A.bypass))
                    em("vector", lambda e: e.tensor_scalar(ISL[:, 5:6], ISL[:, 3:4], 7, None, A.arith_shift_left, A.bypass))
                    em("vector", lambda e: e.tensor_tensor(ISL[:, 5:6], ISL[:, 5:6], ISL[:, 4:5], A.add))
                    em("vector", lambda e: e.tensor_copy(ISL[:, 6:7], UWT[:, 58:59]))
                    em("vector", lambda e: e.tensor_scalar(ISL[:, 7:8], ISL[:, 6:7], 20, None, A.arith_shift_left, A.bypass))
                    em("vector", lambda e, r_=r_: e.tensor_tensor(ISL[:, r_:r_ + 1], ISL[:, 5:6], ISL[:, 7:8], A.add))
                isl_m = last["DVE"]
                for r_ in range(2):
                    a3_sc[i] = em("gpsimd", lambda e, ii=i, r_=r_: e.indirect_dma_start(
                        idxd_flat, IndirectOffsetOnAxis(ISL[:, r_:r_ + 1], 0),
                        TOKID[:, ii:ii + 1], None,
                        bounds_check=BREG["s"], oob_is_err=False),
                        waits=[("DVE", isl_m), ("X", pre_mil)] if r_ == 0 else (),
                        inc=True, key="SC")
            a3_scat_all = cnt["SC"]

            if "B" not in phases:
                prev = {"out_so": cnt["SO"], "d_dve": cnt["DVE"], "c_pe": cnt["PE"],
                        "x_all": cnt["X"], "sc_fin": cnt["SC"], "ga_fin": cnt["GA"]}
                continue

            prev_head_o_last = 0
            prev_head_oevac = mb_m
            attn_store = [0] * H
            for h in range(H):
                s0 = h // 2
                rbase = (h % 2) * 64
                vw_w = [("DVE", w16_last), ("PE", prev_head_o_last), ("X", attn_store[h - 1] if h >= 1 else 0)]
                for j in range(TIL):
                    em("vector", lambda e, j=j, h=h: e.tensor_scalar(
                        GEN[:, VW_C + j * 65:VW_C + j * 65 + 64],
                        BIG[:, j * D + h * HD: j * D + h * HD + 64],
                        W16[:, j * 16 + h: j * 16 + h + 1], None, A.mult, A.bypass),
                        waits=vw_w if j == 0 else ())
                vw_m = em("vector", lambda e, h=h: e.tensor_copy(
                    GEN[:, VW_C:VW_C + 65 * 32].rearrange("p (j c) -> p j c", c=65)[:, :, 64:65],
                    W16[:, 0:512].rearrange("p (j c) -> p j c", c=16)[:, :, h:h + 1]))
                smil = [0] * (NCH * TIL)
                omil = [0] * (NCH * TIL)
                expmil = [0] * (NCH * TIL)
                oevac = [0] * NCH
                PSBF = PSB[:].bitcast(F32)

                def s_ap(jj, hh=None):
                    sl = jj % 3
                    base = (PSBF if sl == 2 else PSF)
                    c0 = 0 if sl == 2 else sl * 1024
                    if hh is None:
                        return base[:, c0:c0 + 1024]
                    return base[:, c0 + hh * 512:c0 + hh * 512 + 512]

                def emit_s(jj, vw_m=vw_m, s0=s0, rbase=rbase, smil=smil, expmil=expmil):
                    ch_, j_ = divmod(jj, TIL)
                    sw = [("ACT", expmil[jj - 3] if jj >= 3 else 0)]
                    if jj == 0:
                        sw += [("DVE", vw_m), ("DVE", gb_m)]
                    for hh in range(2):
                        smil[jj] = em("tensor", lambda e, s0=s0, rb=rbase, j=j_, ch=ch_, jj=jj, hh=hh: e.matmul(
                            s_ap(jj, hh),
                            XNT[rb:rb + 64, s0 * T + j * 128: s0 * T + (j + 1) * 128],
                            XNT[rb:rb + 64, s0 * T + ch * CH + hh * 512: s0 * T + ch * CH + hh * 512 + 512],
                            start=True, stop=True),
                            waits=sw if hh == 0 else (), inc=(hh == 1))
                emit_s(0)
                emit_s(1)
                for jj in range(NCH * TIL):
                    ch, j = divmod(jj, TIL)
                    if jj + 2 < NCH * TIL:
                        emit_s(jj + 2)
                    if jj % 3 == 0 and jj + 1 < NCH * TIL:
                        ew = [("PE", smil[jj + 1]),
                              ("PE", omil[jj - 2] if jj >= 2 else prev_head_o_last)]
                        expmil[jj] = em("scalar", lambda e, jj=jj, h=h: e.activation(
                            GEN[:, RS_C:RS_C + 2048],
                            PSF[:, 0:2048],
                            AF.Exp, bias=GB[:, h: h + 1]),
                            waits=ew)
                        expmil[jj + 1] = expmil[jj]
                    elif jj % 3 == 1:
                        pass
                    else:
                        ew = [("PE", smil[jj]),
                              ("PE", omil[jj - 3] if jj >= 3 else prev_head_o_last)]
                        expmil[jj] = em("scalar", lambda e, jj=jj, h=h: e.activation(
                            GEN[:, RS_C + 2048:RS_C + 3072],
                            s_ap(jj),
                            AF.Exp, bias=GB[:, h: h + 1]),
                            waits=ew)
                    ow = [("ACT", expmil[jj])]
                    if j == 0:
                        ow.append(("DVE", oevac[ch - 1] if ch >= 1 else prev_head_oevac))
                    for hh in range(2):
                        omil[jj] = em("tensor", lambda e, j=j, jj=jj, hh=hh: e.matmul(
                            PSF[0:65, 2048 + hh * 512:2048 + hh * 512 + 512],
                            GEN[:, VW_C + j * 65:VW_C + (j + 1) * 65],
                            GEN[:, RS_C + (jj % 3) * 1024 + hh * 512:RS_C + (jj % 3) * 1024 + hh * 512 + 512],
                            start=(j == 0), stop=(j == TIL - 1)),
                            waits=ow if hh == 0 else (), inc=(hh == 1))
                    if j == TIL - 1:
                        ovw = [("PE", omil[jj])]
                        if ch == 0:
                            ovw.append(("X", attn_store[h - 1] if h >= 1 else 0))
                        oevac[ch] = em("vector", lambda e, ch=ch: e.tensor_copy(
                            GEN[0:65, OST_C + ch * CH:OST_C + (ch + 1) * CH], PSF[0:65, 2048:3072]),
                            waits=ovw)
                prev_head_o_last = omil[NCH * TIL - 1]
                prev_head_oevac = oevac[NCH - 1]
                attn_store[h] = em("sync", lambda e, h=h: e.dma_start(
                    out=attnT[:, h, :], in_=GEN[0:65, OST_C:OST_C + T]),
                    waits=[("DVE", oevac[NCH - 1])], inc=True)
            b_last_pe = prev_head_o_last
            b_last_oevac = prev_head_oevac
            b_store_x = attn_store[H - 1]

            if "C" not in phases:
                prev = {"out_so": cnt["SO"], "d_dve": cnt["DVE"], "c_pe": cnt["PE"],
                        "x_all": cnt["X"], "sc_fin": cnt["SC"], "ga_fin": cnt["GA"]}
                continue

            bp01 = ("DVE", 0)
            bp23 = ("ACT", 0)
            bp45 = ("DVE", b_last_oevac)
            wi_slot_free = [b_last_pe, b_last_pe]
            wo_slot_free = [b_last_pe, b_last_pe]
            xg_tr_done = [b_last_pe] * 4
            xgt_cp = [0, 0]
            ysc_dve = 0
            ysc_par = [0, 0]
            ygc = 0
            sc_chain = prev["sc_fin"]
            p1_last = {"DVE": 0, "ACT": 0}
            p2_last_pe = b_last_pe
            yts_free = [0, 0]
            ytr_free = b_last_pe
            idx_mils = [0, 0]
            for ee in range(E):
                par = ee % 2
                sc_gate = sc_chain
                idx_mils[par] = em("gpsimd", lambda e, ee=ee, par=par: e.dma_start(
                    out=IDXS[:, par * 16:(par + 1) * 16], in_=idxd2d[:, ee * GT:(ee + 1) * GT]),
                    waits=[("SC", a3_scat_all)], inc=True, key="IX")
                g_mils = [0] * GT
                for tl in range(GT):
                    gtl = ee * GT + tl
                    slot = gtl % 4
                    gw = [("IX", idx_mils[par]), ("PE", xg_tr_done[slot]),
                          ("XN", xnd_all), ("SC", cnt["SC"])]
                    if ee == 0:
                        gw.append(("X", b_store_x))
                    if "gather" in skip:
                        g_mils[tl] = em("gpsimd", lambda e, slot=slot, tl=tl: e.dma_start(
                            out=GEN[:, XG_C + slot * XW:XG_C + slot * XW + XW],
                            in_=xnd[tl * 128:(tl + 1) * 128, :]),
                            waits=gw, inc=True, key="GA")
                    else:
                        g_mils[tl] = em("gpsimd", lambda e, slot=slot, par=par, tl=tl: e.indirect_dma_start(
                            GEN[:, XG_C + slot * XW:XG_C + slot * XW + XW], None,
                            xnd, IndirectOffsetOnAxis(IDXS[:, par * 16 + tl:par * 16 + tl + 1], 0),
                            bounds_check=BREG["t"], oob_is_err=False),
                            waits=gw, inc=True, key="GA")
                for tl in range(GT):
                    gtl = ee * GT + tl
                    slot = gtl % 4
                    xtr = 0
                    for g in range(2):
                        gg = gtl * 2 + g
                        tw = [("GA", g_mils[tl]), ("DVE", xgt_cp[(gg - 2) % 2] if gg >= 2 else 0)]
                        if tl == 0 and g == 0:
                            tw.append(("DVE", ysc_dve if ysc_dve else b_last_oevac))
                        for ss4 in range(4):
                            ss = g * 4 + ss4
                            xtr = em("tensor", lambda e, slot=slot, ss=ss, gg=gg, ss4=ss4: e.transpose(
                                PSB[:, (gg % 2) * 1024 + ss4 * 128:(gg % 2) * 1024 + (ss4 + 1) * 128],
                                GEN[:, XG_C + slot * XW + ss * 128:XG_C + slot * XW + (ss + 1) * 128],
                                IDN[:]),
                                waits=tw if ss4 == 0 else (), inc=(ss4 == 3))
                        xgt_cp[gg % 2] = em("vector", lambda e, par=par, tl=tl, gg=gg, g=g: e.tensor_copy(
                            BIG[:, par * 16384:par * 16384 + 16384].rearrange("p (k c) -> p k c", c=CAP)[:, g * 4:(g + 1) * 4, tl * 128:(tl + 1) * 128],
                            PSB[:, (gg % 2) * 1024:(gg % 2) * 1024 + 512].rearrange("p (k c) -> p k c", c=128)),
                            waits=[("PE", xtr)])
                    xg_tr_done[slot] = xtr
                    em("vector", lambda e, par=par, tl=tl, slot=slot, ee=ee: e.tensor_copy(
                        UWT[:, par * 16 + tl:par * 16 + tl + 1],
                        GEN[:, XG_C + slot * XW + 1024 + ee:XG_C + slot * XW + 1024 + ee + 1]))
                xgt_all = last["DVE"]
                for wc in range(NWC):
                    wcsc = sc_chain
                    for ht in range(32):
                        wdm = em("gpsimd", lambda e, ee=ee, ht=ht: e.dma_start(
                            out=GEN[:, WI_C + (ht % 2) * 1024:WI_C + (ht % 2) * 1024 + 1024], in_=wih[ee, ht]),
                            waits=[("PE", wi_slot_free[ht % 2])], inc=True, key="WI")
                        bp = bp01 if ht % 2 == 0 else bp23
                        mw = [("WI", wdm), (bp[0], bp[1]), ("DVE", xgt_all)]
                        if ht < 2:
                            mw.append(("PE", p2_last_pe))
                        p1m = 0
                        for k in range(8):
                            for hh in range(2):
                                p1m = em("tensor", lambda e, ht=ht, k=k, hh=hh, wc=wc, par=par: e.matmul(
                                    PSF[:, (ht % 2) * 1024 + hh * 512:(ht % 2) * 1024 + hh * 512 + 512],
                                    GEN[:, WI_C + (ht % 2) * 1024 + k * 128:WI_C + (ht % 2) * 1024 + (k + 1) * 128],
                                    BIG[:, par * 16384 + k * CAP + wc * 1024 + hh * 512: par * 16384 + k * CAP + wc * 1024 + hh * 512 + 512],
                                    start=(k == 0), stop=(k == 7)),
                                    waits=mw if (k == 0 and hh == 0) else (), inc=(k == 7 and hh == 1))
                        wi_slot_free[ht % 2] = p1m
                        evw = [("PE", p1m)]
                        if ht < 2:
                            evw.append(("PE", p2_last_pe))
                        if ht % 2 == 0:
                            pv = em("vector", lambda e, ht=ht: e.tensor_copy(
                                XNT[:, ht * 1024:(ht + 1) * 1024], PSF[:, 0:1024]), waits=evw)
                            bp01 = ("DVE", pv)
                            p1_last["DVE"] = pv
                        else:
                            pa = em("scalar", lambda e, ht=ht: e.copy(
                                XNT[:, ht * 1024:(ht + 1) * 1024], PSF[:, 1024:2048]), waits=evw)
                            bp23 = ("ACT", pa)
                            p1_last["ACT"] = pa
                    for dt in range(8):
                        wdm = em("gpsimd", lambda e, ee=ee, dt=dt: e.dma_start(
                            out=GEN[:, WO_C + (dt % 2) * 4096:WO_C + (dt % 2) * 4096 + 4096], in_=woh[ee, dt]),
                            waits=[("PE", wo_slot_free[dt % 2])], inc=True, key="WO")
                        bp = bp45 if dt % 2 == 0 else bp23
                        mw = [("WO", wdm), (bp[0], bp[1]),
                              ("DVE", p1_last["DVE"]), ("ACT", p1_last["ACT"])]
                        pcol = 2048 if dt % 2 == 0 else 1024
                        p2m = 0
                        for sth in range(32):
                            for hh in range(2):
                                p2m = em("tensor", lambda e, dt=dt, sth=sth, hh=hh, pcol=pcol: e.matmul(
                                    PSF[:, pcol + hh * 512:pcol + hh * 512 + 512],
                                    GEN[:, WO_C + (dt % 2) * 4096 + sth * 128:WO_C + (dt % 2) * 4096 + (sth + 1) * 128],
                                    XNT[:, sth * 1024 + hh * 512: sth * 1024 + hh * 512 + 512],
                                    start=(sth == 0), stop=(sth == 31)),
                                    waits=mw if (sth == 0 and hh == 0) else (), inc=(sth == 31 and hh == 1))
                        wo_slot_free[dt % 2] = p2m
                        p2_last_pe = p2m
                        ev = em("vector", lambda e, dt=dt, pcol=pcol: e.tensor_copy(
                            GEN[:, YTS_C + (dt % 2) * 1024:YTS_C + (dt % 2) * 1024 + 1024],
                            PSF[:, pcol:pcol + 1024]),
                            waits=[("PE", p2m), ("PE", ytr_free)])
                        if dt % 2 == 0:
                            bp45 = ("DVE", ev)
                        else:
                            bp23 = ("DVE", ev)
                        ytr = 0
                        for st in range(8):
                            if st % 4 == 0:
                                q = ygc % 2
                                ygc += 1
                                gw2 = [("DVE", ysc_par[q])]
                                if st == 0:
                                    gw2.append(("DVE", ev))
                            ytr = em("tensor", lambda e, st=st, q=q, dt=dt: e.transpose(
                                PSB[:, q * 1024 + 512 + (st % 4) * 128:q * 1024 + 512 + (st % 4) * 128 + 128],
                                GEN[:, YTS_C + (dt % 2) * 1024 + st * 128:YTS_C + (dt % 2) * 1024 + (st + 1) * 128],
                                IDN[:]),
                                waits=gw2 if st % 4 == 0 else (), inc=(st % 4 == 3))
                            if st % 4 == 3:
                                for s2 in range(st - 3, st + 1):
                                    yw = [("PE", ytr)] if s2 == st - 3 else []
                                    if dt == 0 and s2 == 0:
                                        yw.append(("SC", wcsc))
                                    ysc_dve = em("vector", lambda e, s2=s2, dt=dt, wc=wc, par=par, q=q: e.tensor_scalar(
                                        YAC[:, s2 * 1024 + dt * 128:s2 * 1024 + (dt + 1) * 128],
                                        PSB[:, q * 1024 + 512 + (s2 % 4) * 128:q * 1024 + 512 + (s2 % 4) * 128 + 128],
                                        UWT[:, par * 16 + wc * 8 + s2:par * 16 + wc * 8 + s2 + 1], None, A.mult, A.bypass),
                                        waits=yw)
                                ysc_par[q] = ysc_dve
                        ytr_free = ytr
                    for st in range(8 if "scatter" not in skip else 0):
                        scw = [("DVE", ysc_dve), ("SC", sc_gate), ("YZ", yz_all),
                               ("IX", idx_mils[par]), ("GA", cnt["GA"])]
                        sc_chain = em("gpsimd", lambda e, st=st, wc=wc, par=par: e.indirect_dma_start(
                            ydr, IndirectOffsetOnAxis(IDXS[:, par * 16 + wc * 8 + st:par * 16 + wc * 8 + st + 1], 0),
                            YAC[:, st * 1024:(st + 1) * 1024], None,
                            bounds_check=BREG["t"], oob_is_err=False, compute_op=A.add),
                            waits=scw if st == 0 else [("SC", sc_gate)],
                            inc=True, key="SC")
            c_last_pe = cnt["PE"]
            c_last_dve = last["DVE"]
            c_sc_fin = sc_chain

            if "D" not in phases:
                prev = {"out_so": cnt["SO"], "d_dve": cnt["DVE"], "c_pe": cnt["PE"],
                        "x_all": cnt["X"], "sc_fin": cnt["SC"], "ga_fin": cnt["GA"]}
                continue

            ain_mil = [0, 0]
            ain_free = [c_last_pe, c_last_pe]
            yin_free = [c_last_pe, c_last_pe]
            acc_free = [0, 0]
            xt_free = [c_last_dve, c_last_dve]
            d_dve = c_last_dve
            atr_last = [0, 0]
            yadd_last = [0, 0]
            AIN_CS = (AIN_C0, AIN_C1)
            YIN_CS = (YIN_C0, YIN_C1)
            for pp in range(16):
                ps = pp % 2
                ain_mil[ps] = em("sync", lambda e, pp=pp, ps=ps: e.dma_start(
                    out=GEN[0:65, AIN_CS[ps]:AIN_CS[ps] + 4096].rearrange("r (h t) -> r h t", t=256),
                    in_=attnT[:, :, pp * 256:(pp + 1) * 256]),
                    waits=[("PE", ain_free[ps]), ("X", b_store_x), ("DVE", c_last_dve)], inc=True)
                for t2 in range(2):
                    i = pp * 2 + t2
                    isl = i % 2
                    em("sync", lambda e, bb=b, ii=i, isl=isl: e.dma_start(
                        out=GEN[:, YIN_CS[isl]:YIN_CS[isl] + 1024],
                        in_=ydr[ii * 128:(ii + 1) * 128, :]),
                        waits=[("SC", c_sc_fin), ("DVE", yadd_last[isl])], inc=True)
                    xm = em("sync", lambda e, bb=b, ii=i, isl=isl: e.dma_start(
                        out=XT[:, isl * 1024:(isl + 1) * 1024], in_=x[bb, ii * 128:(ii + 1) * 128, :]),
                        waits=[("DVE", xt_free[isl])], inc=True)
                    tw = [("X", xm), ("DVE", yadd_last[1 - isl] if i >= 1 else 0),
                          ("DVE", d_dve if i == 0 else 0)]
                    atr = 0
                    for h in range(H):
                        atr = em("tensor", lambda e, h=h, ps=ps, t2=t2: e.transpose(
                            PSB[:, h * 128:h * 128 + 65],
                            GEN[0:65, AIN_CS[ps] + h * 256 + t2 * 128:AIN_CS[ps] + h * 256 + t2 * 128 + 128],
                            IDN[0:65, 0:65]),
                            waits=tw if h == 0 else (), inc=(h == H - 1))
                    atr_last[isl] = atr
                    ain_free[ps] = atr
                    for h in range(H):
                        em("vector", lambda e, h=h: e.reciprocal(RCP[:, h:h + 1], PSB[:, h * 128 + 64:h * 128 + 65]),
                           waits=[("PE", atr)] if h == 0 else ())
                    st_m = 0
                    for h in range(H):
                        st_m = em("vector", lambda e, h=h, isl=isl: e.scalar_tensor_tensor(
                            U2[:, isl * 1024 + h * 64: isl * 1024 + (h + 1) * 64],
                            PSB[:, h * 128:h * 128 + 64], RCP[:, h:h + 1],
                            XT[:, isl * 1024 + h * 64: isl * 1024 + (h + 1) * 64], A.mult, A.add),
                            waits=[("X", xm), ("SO", acc_free[isl])] if h == 0 else ())
                    xt_free[isl] = st_m
                    ya = em("vector", lambda e, isl=isl: e.tensor_tensor(
                        U2[:, isl * 1024:(isl + 1) * 1024], U2[:, isl * 1024:(isl + 1) * 1024],
                        GEN[:, YIN_CS[isl]:YIN_CS[isl] + 1024], A.add),
                        waits=[("X", xm)])
                    yadd_last[isl] = ya
                    d_dve = ya
                    acc_free[isl] = em("sync", lambda e, bb=b, ii=i, isl=isl: e.dma_start(
                        out=out[bb, ii * 128:(ii + 1) * 128, :], in_=U2[:, isl * 1024:(isl + 1) * 1024]),
                        waits=[("DVE", ya)], inc=True, key="SO")
            if b + 1 < NB:
                pre_mil = em("sync", lambda e: e.dma_start(out=idxd2d, in_=OOBT[:]),
                             waits=[("GA", cnt["GA"]), ("IX", cnt["IX"])], inc=True)
            prev = {"out_so": acc_free[1], "d_dve": d_dve, "c_pe": cnt["PE"],
                    "x_all": cnt["X"], "sc_fin": c_sc_fin, "ga_fin": cnt["GA"]}

        for k in DMAK:
            if cnt[k]:
                eng = "sync" if k in ("X", "SO") else "gpsimd"
                em(eng, lambda e: e.nop(), waits=[(k, cnt[k])])

        @block.sync
        def _(eng):
            for f in plan["sync"]:
                f(eng, sems)

        @block.gpsimd
        def _(eng):
            for f in plan["gpsimd"]:
                f(eng, sems)

        @block.tensor
        def _(eng):
            for f in plan["tensor"]:
                f(eng, sems)

        @block.scalar
        def _(eng):
            for f in plan["scalar"]:
                f(eng, sems)

        @block.vector
        def _(eng):
            for f in plan["vector"]:
                f(eng, sems)

    return nc


def host_inputs(L0_attn_x, L0_ffn_w_gate, L0_ffn_w_in, L0_ffn_w_out):
    xf = np.ascontiguousarray(np.asarray(L0_attn_x, dtype=np.float32))
    wg = np.asarray(L0_ffn_w_gate, dtype=np.float32)
    wi = np.asarray(L0_ffn_w_in, dtype=np.float32)
    wo = np.asarray(L0_ffn_w_out, dtype=np.float32)
    WIH = np.ascontiguousarray(wi.reshape(E, 32, 128, 8, 128).transpose(0, 1, 4, 3, 2)).reshape(E, 32, 128, 1024)
    WOH = np.ascontiguousarray(wo.reshape(E, 8, 128, 32, 128).transpose(0, 1, 4, 3, 2)).reshape(E, 8, 128, 4096)
    WGH = np.ascontiguousarray(wg.reshape(8, 8, 128).transpose(2, 1, 0)).reshape(128, 64)
    ident = np.eye(128, dtype=np.float32)
    cones = np.ones((128, 1), dtype=np.float32)
    crow = np.ones((1, 128), dtype=np.float32)
    cnrow = -np.ones((1, 128), dtype=np.float32)
    ide8 = np.zeros((40, 8), dtype=np.float32)
    ide8[0:8] = np.eye(8); ide8[32:40] = np.eye(8)
    crowf = np.ones((1, 128), dtype=np.float32)
    shards = [np.ascontiguousarray(xf[2 * c:2 * c + 2]) for c in range(8)]
    in_maps = [{"x": shards[c], "wih": WIH, "woh": WOH, "wgh": WGH, "ident": ident,
                "cones": cones, "crow": crow, "cnrow": cnrow, "ide8": ide8, "crowf": crowf} for c in range(8)]
    return in_maps


def kernel(L0_attn_x, L0_ffn_w_gate, L0_ffn_w_in, L0_ffn_w_out):
    in_maps = host_inputs(L0_attn_x, L0_ffn_w_gate, L0_ffn_w_in, L0_ffn_w_out)
    nc = build_program()
    res = run_bass_kernel_spmd(nc, in_maps, core_ids=list(range(8)))
    return np.concatenate([np.asarray(r["out"]) for r in res.results], axis=0)
